# version 19
# baseline (speedup 1.0000x reference)
import contextlib
import numpy as np
import ml_dtypes

import concourse.bass as bass
import concourse.mybir as mybir
from concourse.bass_utils import run_bass_kernel_spmd

F32 = mybir.dt.float32
BF16 = mybir.dt.bfloat16
AF = mybir.ActivationFunctionType
ALU = mybir.AluOpType
AX = mybir.AxisListType
BF = ml_dtypes.bfloat16

T = 4096
D = 1024
H = 16
HD = 64
NT = 32
NS = 8
NCORE = 8
DEBUG_TAPS = False
STRICT_EDGES = False
CLIPF = float(np.float32(np.float32(128.0 - 1e-5) / np.float32(128.0)))


class Ctr:

    def __init__(self, es, nc, name):
        self.h = es.enter_context(nc.semaphore(name))
        self.n = 0

    def bump(self, k=1):
        self.n += k
        return self.n


def _host_consts():
    inv_freq = (1.0 / np.float32(10000.0) ** (
        np.arange(0, HD, 2, dtype=np.float32) / np.float32(HD))).astype(np.float32)
    t = np.arange(T, dtype=np.float32)
    freqs = (t[:, None] * inv_freq[None, :]).astype(np.float32)
    cosv = np.cos(freqs).astype(np.float32)
    sinv = np.sin(freqs).astype(np.float32)
    cc = np.empty((128, T), dtype=np.float32)
    ss = np.empty((128, T), dtype=np.float32)
    for j in range(128):
        f = j % 32
        cc[j] = cosv[:, f]
        ss[j] = sinv[:, f] if j < 64 else (-sinv[:, f])
    perm = np.empty(D, dtype=np.int64)
    for c in range(8):
        for j in range(128):
            sub = (j % 64) // 32
            half = j // 64
            f = j % 32
            perm[c * 128 + j] = (2 * c + sub) * HD + half * 32 + f
    sel = np.zeros((16, D), dtype=BF)
    for c in range(8):
        for m in range(128):
            h = 2 * c + (m // 32) % 2
            sel[h, c * 128 + m] = 1.0
    mask2 = np.zeros((128, 2), dtype=np.float32)
    for p in range(128):
        mask2[p, 0] = 1.0 if (p // 32) % 2 == 0 else 0.0
        mask2[p, 1] = 1.0 - mask2[p, 0]
    ident_b = np.eye(128, dtype=BF)
    ident_f = np.eye(128, dtype=np.float32)
    return cc.astype(BF), ss.astype(BF), perm, sel, mask2, ident_b, ident_f


def build_nc():
    nc = bass.Bass("TRN2", target_bir_lowering=False)

    x_in = nc.dram_tensor("x_in", [T, D], F32, kind="ExternalInput").ap()
    wqkvT = nc.dram_tensor("wqkvT", [D, 3 * D], F32, kind="ExternalInput").ap()
    woutT = nc.dram_tensor("woutT", [D, D], F32, kind="ExternalInput").ap()
    wffn = nc.dram_tensor("wffn", [D, D], F32, kind="ExternalInput").ap()
    gam_in = nc.dram_tensor("gam_in", [1, D], F32, kind="ExternalInput").ap()
    bet_in = nc.dram_tensor("bet_in", [1, D], F32, kind="ExternalInput").ap()
    cc_in = nc.dram_tensor("cc_in", [128, T], BF16, kind="ExternalInput").ap()
    ss_in = nc.dram_tensor("ss_in", [128, T], BF16, kind="ExternalInput").ap()
    sel_in = nc.dram_tensor("sel_in", [16, D], BF16, kind="ExternalInput").ap()
    mask_in = nc.dram_tensor("mask_in", [128, 2], F32, kind="ExternalInput").ap()
    idb_in = nc.dram_tensor("idb_in", [128, 128], BF16, kind="ExternalInput").ap()
    idf_in = nc.dram_tensor("idf_in", [128, 128], F32, kind="ExternalInput").ap()
    y_out = nc.dram_tensor("y_out", [T, D], F32, kind="ExternalOutput").ap()
    dbg = {}
    if DEBUG_TAPS:
        for nm, shape, dt in [
            ("dbg_xn1", [128, D], BF16), ("dbg_z00", [128, 512], BF16),
            ("dbg_qf0", [128, T], BF16), ("dbg_kv", [64, D], BF16),
            ("dbg_ks", [128, 8], F32), ("dbg_Kb", [128, 128], BF16),
            ("dbg_M0", [128, D], BF16), ("dbg_rec", [16, 512], F32),
            ("dbg_x2", [128, D], F32), ("dbg_xn2", [128, D], BF16),
            ("dbg_gm", [1, 1], F32), ("dbg_ktm", [128, 4096], BF16),
            ("dbg_qs", [128, 4096], BF16), ("dbg_xr", [128, D], F32),
            ("dbg_wq0", [128, D], BF16), ("dbg_xqfm", [128, D], BF16),
            ("dbg_xn2all", [128, 32768], BF16),
            ("dbg_vtm", [128, 4096], BF16), ("dbg_fm", [128, 4096], BF16),
        ]:
            dbg[nm] = nc.dram_tensor(nm, shape, dt, kind="ExternalOutput").ap()
    x2_scr = nc.dram_tensor("x2_scr", [T, D], F32).ap()
    ccoll_i = nc.dram_tensor("ccoll_i", [1, 512], F32)
    ccoll_o = nc.dram_tensor("ccoll_o", [1, 512], F32)

    es = contextlib.ExitStack()
    with es:
        sq = Ctr(es, nc, "sq")
        gq = Ctr(es, nc, "gq")
        scc = Ctr(es, nc, "scc")
        sv = Ctr(es, nc, "sv")
        sa = Ctr(es, nc, "sa")
        sp = Ctr(es, nc, "sp")
        sg = Ctr(es, nc, "sg")
        sx = [Ctr(es, nc, f"sx{i}") for i in range(2)]
        stb = Ctr(es, nc, "stb")
        sxr = [Ctr(es, nc, f"sxr{i}") for i in range(2)]
        sx2r = [Ctr(es, nc, f"sx2r{i}") for i in range(2)]
        gw = [Ctr(es, nc, f"gw{i}") for i in range(2)]
        gy = [Ctr(es, nc, f"gy{i}") for i in range(2)]
        swm = Ctr(es, nc, "swm")
        swc = Ctr(es, nc, "swc")
        _eng_ctr_map = {}

        def sb(name, shape, dt):
            return es.enter_context(nc.sbuf_tensor(name, shape, dt))

        M_sb = [sb(f"M_sb{c}", [128, D], BF16) for c in range(8)]
        ident_b = sb("identb", [128, 128], BF16)
        ident_f = sb("identf", [128, 128], F32)
        ones_col = sb("ones_col", [128, 1], BF16)
        ones_f = sb("ones_f", [1, 128], F32)
        mask_sb = sb("mask2s", [128, 2], F32)
        Kb = sb("Kb", [128, 128], BF16)
        gmax = sb("gmax", [128, 1], F32)
        gm_s = sb("gm_s", [1, 1], F32)
        gcol = sb("gcol", [128, 1], F32)
        gneg = sb("gneg", [128, 1], F32)
        bcol = sb("bcol", [128, 1], F32)
        rbcol = sb("rbcol", [128, 1], F32)

        block = es.enter_context(nc.Block())
        ENG = {"sync": block.sync, "vector": block.vector, "scalar": block.scalar,
               "tensor": block.tensor, "gpsimd": block.gpsimd}

        def on(eng, fn):
            ENG[eng](fn)
        _eng_ctr_map.update({"vector": sv, "scalar": sa, "gpsimd": sg})

        def _pin_tables(e):
            e.add_instruction(mybir.InstLoadActFuncSet(
                name=nc.get_next_instruction_name(),
                act_func_set_id=6, ins=[], outs=[]))
        on("scalar", _pin_tables)

        def dma(eng, ctr, out, in_, waits=()):
            def body(e):
                for (c, v) in waits:
                    e.wait_ge(c.h, v)
                e.dma_start(out=out, in_=in_).then_inc(ctr.h, 16)
            on(eng, body)
            ctr.bump(16)

        ENG_CTR = _eng_ctr_map

        def emit(eng, ctr, fn, waits=(), inc=1, fast=False):
            self_ctr = ENG_CTR.get(eng)
            if self_ctr is not None:
                if ctr is None:
                    ctr = self_ctr
                assert ctr is self_ctr, "serializing engines must inc their own ctr"
                selfw = [] if (fast and not STRICT_EDGES) else [(self_ctr, self_ctr.n)]
            else:
                selfw = []

            def body(e):
                for (c, v) in selfw:
                    e.wait_ge(c.h, v)
                for (c, v) in waits:
                    e.wait_ge(c.h, v)
                ins = fn(e)
                if ctr is not None and ins is not None:
                    ins.then_inc(ctr.h, inc)
            on(eng, body)
            if ctr is not None:
                ctr.bump(inc)

        with contextlib.ExitStack() as es_qf:
            qf = [es_qf.enter_context(nc.sbuf_tensor(f"qf{c}", [128, T], BF16))
                  for c in range(8)]

            with contextlib.ExitStack() as es_p1:
                def sb1(name, shape, dt):
                    return es_p1.enter_context(nc.sbuf_tensor(name, shape, dt))

                wqkvT_bf = [sb1(f"wqkvTb{c}", [128, 3 * D], BF16) for c in range(8)]
                tab_cc = sb1("tab_cc", [128, 512], BF16)
                tab_ss = sb1("tab_ss", [128, 512], BF16)
                xbuf = [sb1(f"xbuf{i}", [128, D], F32) for i in range(2)]
                xdma_done = {}
                tabs = {}
                for g0 in range(2):
                    dma("sync", sx[g0], xbuf[g0][:], x_in[g0 * 128:(g0 + 1) * 128, :])
                    xdma_done[g0] = (sx[g0], sx[g0].n)
                dma("sync", stb, tab_cc[:], cc_in[:, 0:512])
                dma("sync", stb, tab_ss[:], ss_in[:, 0:512])
                tabs[0] = stb.n
                dma("sync", sq, ident_b[:], idb_in)
                dma("sync", sq, ident_f[:], idf_in)
                dma("sync", sq, mask_sb[:], mask_in)
                const_ready = sq.n
                emit("vector", sv, lambda e: e.memset(ones_col[:], 1.0))
                emit("vector", sv, lambda e: e.memset(ones_f[:], 1.0))
                emit("vector", sv, lambda e: e.memset(gmax[:], 0.0))
                emit("vector", sv, lambda e: e.memset(Kb[:], 0.0))
                for c in range(8):
                    dma("gpsimd", swc, wqkvT_bf[c][:, :], wqkvT[c * 128:(c + 1) * 128, :])
                for c in range(8):
                    emit("vector", sv, lambda e, c=c: e.tensor_scalar(
                        wqkvT_bf[c][:, 0:D], wqkvT_bf[c][:, 0:D], 0.125, None, ALU.mult),
                        waits=[(swc, swc.n)] if c == 0 else [])
                wqkv_ready_v = sv.n
                wqkv_ready = swc.n

                woutc = [sb1(f"woutc{i}", [64, 2 * D], BF16) for i in range(2)]
                xn1_tm = [sb1(f"xn1tm{i}", [128, D], BF16) for i in range(4)]
                xn1_fm = [sb1(f"xn1fm{i}", [128, 4096], BF16) for i in range(2)]
                ktm = sb1("ktm", [128, 4096], BF16)
                vtm = sb1("vtm", [128, 4096], BF16)
                zsrc = [sb1(f"zsrc{i}", [128, 512], BF16) for i in range(2)]
                t1b = [sb1(f"t1b{i}", [128, 512], BF16) for i in range(2)]
                t2b = [sb1(f"t2b{i}", [128, 512], BF16) for i in range(2)]
                zb = [sb1(f"zb{i}", [128, 512], BF16) for i in range(2)]
                kfm = [sb1(f"kfm{i}", [128, 512], BF16) for i in range(2)]
                kv_sb = sb1("kv_sb", [64, D], BF16)
                ks_sb = sb1("ks_sb", [128, 8], F32)
                st12 = sb1("st12", [128, 12], F32)
                mv = sb1("mv", [128, 2], F32)
                veps = sb1("veps", [128, 1], F32)
                rstd_a = sb1("rstd_a", [128, 1], F32)
                rstd_b = sb1("rstd_b", [128, 1], F32)
                rstd = sb1("rstd", [128, 1], F32)

                ps_qk = [es_p1.enter_context(nc.psum_tensor(f"psqk{i}", [128, 512], F32))
                         for i in range(2)]
                ps_v = [es_p1.enter_context(nc.psum_tensor(f"psv{i}", [128, 512], F32))
                        for i in range(2)]
                ps_T = es_p1.enter_context(nc.psum_tensor("psT", [128, 1024], BF16))
                ps_vk = es_p1.enter_context(nc.psum_tensor("psvk", [64, 1024], F32))
                ps_ks = es_p1.enter_context(nc.psum_tensor("psks", [128, 8], F32))

                xn1fm_view = [f.ap().rearrange("p (c t j) -> p c t j", c=8, t=4)
                              for f in xn1_fm]
                ktm_view = ktm.ap().rearrange("p (t c j) -> p t c j", t=4, c=8)
                ktm_head = ktm.ap().rearrange("p (t c h s f) -> p t c h s f",
                                              t=4, c=8, h=2, s=2)
                psT_view8 = ps_T.ap().rearrange("p (c j) -> p c j", c=8)
                psT_view4 = ps_T[:, 0:512].rearrange("p (t j) -> p t j", t=4)

                ln_done = {}
                fmev = {}
                b4_free = [sv.n]
                zev = {}
                zsrc_free = [sv.n, sv.n]
                vev = {}
                kT_mm = {}
                T_mm = {}
                ktmv = {}
                qk_mm = {}
                v_mm = {}
                last_rope_sv = [0]

                def emit_ln(g):
                    bi = g % 4
                    xi = g % 2
                    emit("vector", sv, lambda e, xi=xi: (
                        e.bn_stats(st12[:, 0:6], xbuf[xi][:, 0:512]),
                        e.bn_stats(st12[:, 6:12], xbuf[xi][:, 512:1024]))[-1],
                        waits=[xdma_done[g]])
                    emit("vector", sv, lambda e: e.bn_aggr(mv[:], st12[:]),
                         waits=[(sv, sv.n)])
                    emit("vector", sv, lambda e: e.tensor_scalar(
                        veps[:], mv[:, 1:2], 1e-5, None, ALU.add), waits=[(sv, sv.n)])
                    emit("scalar", sa, lambda e: e.activation(rstd_a[:], veps[:], AF.Ln),
                         waits=[(sv, sv.n)])
                    emit("scalar", sa,
                         lambda e: e.activation(rstd_b[:], rstd_a[:], AF.Exp, scale=-0.5),
                         waits=[(sa, sa.n)])
                    emit("vector", sv, lambda e: e.tensor_copy(rstd[:], rstd_b[:]),
                         waits=[(sa, sa.n)])
                    aff_w = [(sv, sv.n)]
                    if (g - 4) in T_mm:
                        aff_w.append((sp, T_mm[g - 4]))
                    emit("vector", sv, lambda e, bi=bi, xi=xi: e.tensor_scalar(
                        xn1_tm[bi][:], xbuf[xi][:], mv[:, 0:1], rstd[:],
                        ALU.subtract, ALU.mult), waits=aff_w)
                    ln_done[g] = sv.n
                    if g + 2 < NT:
                        dma("sync", sx[g % 2], xbuf[(g + 2) % 2][:],
                            x_in[(g + 2) * 128:(g + 3) * 128, :],
                            waits=[(sv, ln_done[g])])
                        xdma_done[g + 2] = (sx[g % 2], sx[g % 2].n)

                def emit_T(g):
                    bi = g % 4
                    def tr(e, bi=bi):
                        ins = None
                        for c in range(8):
                            ins = e.transpose(ps_T[:, c * 128:(c + 1) * 128],
                                              xn1_tm[bi][:, c * 128:(c + 1) * 128],
                                              ident_b[:])
                        return ins
                    tw = [(sv, ln_done[g]), (sv, b4_free[0])]
                    if g == 0:
                        tw.append((sq, const_ready))
                    emit("tensor", sp, tr, waits=tw)
                    T_mm[g] = sp.n
                    tt = g % 4
                    fb = (g // 4) % 2
                    emit("vector", sv, lambda e, tt=tt, fb=fb: e.tensor_copy(
                        xn1fm_view[fb][:, :, tt, :], psT_view8[:, :, :]),
                        waits=[(sp, T_mm[g])], fast=True)
                    fmev[g] = sv.n
                    b4_free[0] = sv.n

                emit_ln(0)
                if DEBUG_TAPS:
                    dma("sync", sq, dbg["dbg_xn1"], xn1_tm[0][:],
                        waits=[(sv, ln_done[0])])
                for g0 in (1, 2, 3):
                    emit_ln(g0)
                for g0 in (0, 1, 2, 3):
                    emit_T(g0)


                featv = {}
                for s in range(NS):
                    fm_ready = max(fmev[4 * s + t] for t in range(4))

                    def emit_v(tt, s=s):
                        g = 4 * s + tt
                        waits = [(sv, fm_ready), (swc, wqkv_ready)]
                        if (g - 1) in vev:
                            waits.append((sa, vev[g - 1]))

                        def vmm2(e, tt=tt, s=s):
                            ins = None
                            for half in range(2):
                                for kchunk in range(8):
                                    ins = e.matmul(
                                        ps_v[half][:, :],
                                        xn1_fm[s % 2][:, kchunk * 512 + tt * 128:
                                                      kchunk * 512 + (tt + 1) * 128],
                                        wqkvT_bf[kchunk][:, 2 * D + half * 512:
                                                         2 * D + (half + 1) * 512],
                                        start=(kchunk == 0), stop=(kchunk == 7),
                                        skip_group_check=True)
                            return ins
                        emit("tensor", sp, vmm2, waits=waits)
                        v_mm[g] = sp.n
                        emit("scalar", None, lambda e, tt=tt: e.copy(
                            vtm[:, tt * 1024: tt * 1024 + 512], ps_v[0][:, :]),
                            waits=[(sp, v_mm[g])], fast=True)
                        emit("scalar", sa, lambda e, tt=tt: e.copy(
                            vtm[:, tt * 1024 + 512: tt * 1024 + 1024], ps_v[1][:, :]),
                            fast=True)
                        vev[g] = sa.n

                    def emit_kT(c, s=s):
                        def ktr(e, c=c):
                            ins = None
                            for p in range(4):
                                ins = e.transpose(ps_T[:, p * 128:(p + 1) * 128],
                                                  kfm[c % 2][:, p * 128:(p + 1) * 128],
                                                  ident_b[:])
                            return ins
                        emit("tensor", sp, ktr,
                             waits=[(sv, featv[(s, c + 8)]), (sv, b4_free[0])])
                        kT_mm[(s, c)] = sp.n
                        emit("vector", sv, lambda e, c=c: e.tensor_copy(
                            ktm_view[:, :, c, :], psT_view4[:, :, :]),
                            waits=[(sp, kT_mm[(s, c)])], fast=True)
                        ktmv[(s, c)] = sv.n
                        b4_free[0] = sv.n

                    for kc in range(16):
                        if kc % 4 == 3 and s < NS - 1:
                            emit_ln(4 * (s + 1) + kc // 4)
                        cB = kc % 2
                        c = kc if kc < 8 else kc - 8
                        boff = 0 if kc < 8 else D
                        prev = (s, kc - 2) if kc >= 2 else \
                               ((s - 1, 14 + kc) if s > 0 else None)
                        waits = [(sv, fm_ready)]
                        if s == 0 and kc == 0:
                            waits.append((swc, wqkv_ready))
                            waits.append((sv, wqkv_ready_v))
                        if prev is not None:
                            waits.append((sa, zev[prev]))

                        def qkmm(e, cB=cB, c=c, boff=boff, s=s):
                            ins = None
                            for kchunk in range(8):
                                ins = e.matmul(
                                    ps_qk[cB][:, :],
                                    wqkvT_bf[kchunk][:, boff + c * 128: boff + (c + 1) * 128],
                                    xn1_fm[s % 2][:, kchunk * 512:(kchunk + 1) * 512],
                                    start=(kchunk == 0), stop=(kchunk == 7),
                                    skip_group_check=True)
                            return ins
                        emit("tensor", sp, qkmm, waits=waits)
                        qk_mm[(s, kc)] = sp.n
                        if kc >= 10:
                            emit_kT(kc - 10)
                        if kc % 4 == 2:
                            emit_v(kc // 4)
                        if kc % 4 == 2 and kc >= 6 and s < NS - 1:
                            emit_T(4 * (s + 1) + kc // 4 - 1)
                        pz = kc % 2
                        emit("scalar", sa, lambda e, cB=cB, pz=pz: e.copy(
                            zsrc[pz][:], ps_qk[cB][:, :]),
                            waits=[(sp, qk_mm[(s, kc)]), (sv, zsrc_free[pz])], fast=True)
                        zev[(s, kc)] = sa.n
                        emit("vector", None,
                             lambda e, pz=pz: e.tensor_tensor(t1b[pz][:], zsrc[pz][:],
                                                              tab_cc[:], ALU.mult),
                             waits=[(sa, zev[(s, kc)]), (stb, tabs[s])], fast=True)
                        emit("vector", None, lambda e, pz=pz: e.tensor_tensor(
                            t2b[pz][0:64, :], zsrc[pz][64:128, :], tab_ss[64:128, :],
                            ALU.mult), fast=True)
                        emit("vector", sv, lambda e, pz=pz: e.tensor_tensor(
                            t2b[pz][64:128, :], zsrc[pz][0:64, :], tab_ss[0:64, :],
                            ALU.mult), fast=True)
                        zsrc_free[pz] = sv.n
                        emit("vector", sv,
                             lambda e, pz=pz: e.tensor_tensor(zb[pz][:], t1b[pz][:],
                                                              t2b[pz][:], ALU.add),
                             fast=True)
                        emit("vector", sv, lambda e, pz=pz: e.tensor_scalar(
                            t1b[pz][:], zb[pz][:], 0.0, None, ALU.min), fast=True)
                        last_rope_sv[0] = sv.n
                        emit("scalar", sa, lambda e, pz=pz: e.activation(
                            t2b[pz][:], t1b[pz][:], AF.Exp), waits=[(sv, sv.n)], fast=True)
                        dest = qf[c][:, s * 512:(s + 1) * 512] if kc < 8 else kfm[c % 2][:, :]
                        kt_wait = [(sa, sa.n)]
                        if kc >= 8:
                            if (s, c - 2) in kT_mm:
                                kt_wait.append((sp, kT_mm[(s, c - 2)]))
                            elif (s - 1, 6 + c) in kT_mm:
                                kt_wait.append((sp, kT_mm[(s - 1, 6 + c)]))
                        emit("vector", sv, lambda e, dest=dest, pz=pz: e.scalar_tensor_tensor(
                            dest, zb[pz][:], 0.0, t2b[pz][:], ALU.max, ALU.add),
                            waits=kt_wait, fast=True)
                        featv[(s, kc)] = sv.n
                    if s < NS - 1:
                        dma("sync", stb, tab_cc[:], cc_in[:, (s + 1) * 512:(s + 2) * 512],
                            waits=[(sv, last_rope_sv[0])])
                        dma("sync", stb, tab_ss[:], ss_in[:, (s + 1) * 512:(s + 2) * 512])
                        tabs[s + 1] = stb.n
                    emit_kT(6)
                    emit_kT(7)
                    if s < NS - 1:
                        emit_T(4 * (s + 1) + 3)
                    if DEBUG_TAPS and s == 0:
                        dma("sync", sq, dbg["dbg_ktm"], ktm[:], waits=[(sv, max(
                            ktmv[(0, cc0)] for cc0 in range(8)))])
                        dma("sync", sq, dbg["dbg_vtm"], vtm[:], waits=[(sa, vev[3])])
                    ktm_ready = max(ktmv[(s, c)] for c in range(8))
                    for tt in range(4):
                        g = 4 * s + tt

                        def vkmm(e, tt=tt, s=s):
                            ins = None
                            for h in range(H):
                                c, sub = h // 2, h % 2
                                ins = e.matmul(
                                    ps_vk[0:64, h * 64:(h + 1) * 64],
                                    vtm[:, tt * 1024 + h * 64: tt * 1024 + (h + 1) * 64],
                                    ktm_head[:, tt, c, :, sub, :],
                                    start=(s == 0 and tt == 0 and h % 8 == 0),
                                    stop=(s == NS - 1 and tt == 3),
                                    skip_group_check=True)
                            for c in range(8):
                                ins = e.matmul(
                                    ps_ks[:, c:c + 1],
                                    ktm[:, tt * 1024 + c * 128: tt * 1024 + (c + 1) * 128],
                                    ones_col[:],
                                    start=(s == 0 and tt == 0 and c == 0),
                                    stop=(s == NS - 1 and tt == 3),
                                    skip_group_check=True)
                            return ins
                        emit("tensor", sp, vkmm, waits=[(sa, vev[g]), (sv, ktm_ready)])

                vk_done = sp.n
                emit("vector", sv, lambda e: e.tensor_copy(kv_sb[:], ps_vk[0:64, :]),
                     waits=[(sp, vk_done)])
                emit("vector", sv, lambda e: e.tensor_copy(ks_sb[:], ps_ks[:, :]))
                ks_ready = sv.n
                for c in range(8):
                    emit("vector", sv, lambda e, c=c: e.tensor_scalar(
                        Kb[:, c * 16 + 2 * c: c * 16 + 2 * c + 2], mask_sb[:, :],
                        ks_sb[:, c:c + 1], None, ALU.mult),
                        waits=[(sv, ks_ready), (sq, const_ready)])
                kb_ready = sv.n
                kv_ready = kb_ready

                wout_dma = {}
                mm_done_c = {}
                for c in range(2):
                    dma("gpsimd", swm, woutc[c % 2][0:64, 0:D],
                        woutT[c * 128: c * 128 + 64, :])
                    dma("gpsimd", swm, woutc[c % 2][0:64, D:2 * D],
                        woutT[c * 128 + 64: c * 128 + 128, :])
                    wout_dma[c] = swm.n
                for c in range(8):
                    if c + 2 < 8:
                        w = [(sp, mm_done_c[c])] if c in mm_done_c else []
                    h1, h2 = 2 * c, 2 * c + 1

                    def mmm(e, h1=h1, h2=h2, c=c):
                        ins = None
                        for nh in range(2):
                            pv = ps_v[nh]
                            pq = ps_qk[nh]
                            N1 = slice(nh * 512, (nh + 1) * 512)
                            N2 = slice(D + nh * 512, D + (nh + 1) * 512)
                            wc = woutc[c % 2]
                            ins = e.matmul(pv[0:32, :], kv_sb[0:64, h1 * 64: h1 * 64 + 32],
                                           wc[0:64, N1], start=True, stop=True,
                                           skip_group_check=True)
                            ins = e.matmul(pv[32:64, :], kv_sb[0:64, h2 * 64: h2 * 64 + 32],
                                           wc[0:64, N2], start=True, stop=True,
                                           skip_group_check=True)
                            ins = e.matmul(pv[64:96, :], kv_sb[0:64, h1 * 64 + 32: h1 * 64 + 64],
                                           wc[0:64, N1], start=True, stop=True,
                                           skip_group_check=True)
                            ins = e.matmul(pq[0:32, :], kv_sb[0:64, h2 * 64 + 32: h2 * 64 + 64],
                                           wc[0:64, N2], start=True, stop=True,
                                           skip_group_check=True)
                        return ins
                    emit("tensor", sp, mmm,
                         waits=[(swm, swm.n), (sv, sv.n)])
                    mdone = sp.n
                    mm_done_c[c] = sp.n
                    if c + 2 < 8:
                        dma("gpsimd", swm, woutc[c % 2][0:64, 0:D],
                            woutT[(c + 2) * 128: (c + 2) * 128 + 64, :],
                            waits=[(sp, mm_done_c[c])])
                        dma("gpsimd", swm, woutc[c % 2][0:64, D:2 * D],
                            woutT[(c + 2) * 128 + 64: (c + 2) * 128 + 128, :])
                        wout_dma[c + 2] = swm.n
                    emit("vector", None, lambda e, c=c: e.tensor_copy(
                        M_sb[c][0:96, 0:512], ps_v[0][0:96, :]), waits=[(sp, mdone)])
                    emit("vector", None, lambda e, c=c: e.tensor_copy(
                        M_sb[c][0:96, 512:1024], ps_v[1][0:96, :]))
                    emit("vector", None, lambda e, c=c: e.tensor_copy(
                        M_sb[c][96:128, 0:512], ps_qk[0][0:32, :]))
                    emit("vector", sv, lambda e, c=c: e.tensor_copy(
                        M_sb[c][96:128, 512:1024], ps_qk[1][0:32, :]))

                if DEBUG_TAPS:
                    dma("sync", sq, dbg["dbg_qf0"], qf[0][:], waits=[(sv, sv.n)])
                    dma("sync", sq, dbg["dbg_kv"], kv_sb[:], waits=[(sv, kv_ready)])
                    dma("sync", sq, dbg["dbg_ks"], ks_sb[:], waits=[(sv, kv_ready)])
                    dma("sync", sq, dbg["dbg_Kb"], Kb[:], waits=[(sv, kb_ready)])
                    dma("sync", sq, dbg["dbg_M0"], M_sb[0][:], waits=[(sv, sv.n)])

            es_x = contextlib.ExitStack()
            xn2 = es_x.enter_context(nc.sbuf_tensor("xn2", [128, NT * 1024], BF16))
            wq_bf = [es_x.enter_context(nc.sbuf_tensor(f"wq_bf{c}", [128, D], BF16))
                     for c in range(8)]
            with contextlib.ExitStack() as es_p2:
                def sb2(name, shape, dt):
                    return es_p2.enter_context(nc.sbuf_tensor(name, shape, dt))

                xr = [sb2(f"xr{i}", [128, D], F32) for i in range(2)]
                x2sb = [sb2(f"x2sb{i}", [128, D], F32) for i in range(2)]
                usb = [sb2(f"usb{i}", [128, D], F32) for i in range(2)]
                qs = sb2("qs", [128, 4096], BF16)
                wqa = sb2("wqa", [128, D], BF16)
                wqb2 = sb2("wqb2", [128, D], BF16)
                asum = sb2("asum", [128, 1], F32)
                wabs = sb2("wabs", [128, 1], F32)
                sum_s = sb2("sum_s", [1, 1], F32)
                beta_s = sb2("beta_s", [1, 1], F32)
                rbeta_s = sb2("rbeta_s", [1, 1], F32)
                dh = sb2("dh", [16, 512], F32)
                recb = sb2("recb", [16, 512], BF16)
                sel_sb = sb2("sel_sb", [16, D], BF16)
                gam_rep = sb2("gam_rep", [128, D], BF16)
                bet_rep = sb2("bet_rep", [128, D], BF16)
                st12b = sb2("st12b", [128, 12], F32)
                mv2 = [sb2(f"mv2_{i}", [128, 2], F32) for i in range(2)]
                veps2 = [sb2(f"veps2_{i}", [128, 1], F32) for i in range(2)]
                c_a = sb2("c_a", [128, 1], F32)
                c_b = [sb2(f"c_b{i}", [128, 1], F32) for i in range(2)]
                c_col = [sb2(f"c_col{i}", [128, 1], F32) for i in range(2)]
                rmx = sb2("rmx", [128, 1], F32)

                ps_d = es_p2.enter_context(nc.psum_tensor("ps_d", [16, 512], F32))
                ps_r = [es_p2.enter_context(nc.psum_tensor(f"ps_r{i}", [128, 512], F32))
                        for i in range(2)]
                ps_p = [es_p2.enter_context(nc.psum_tensor(f"ps_p{i}", [128, 1024], F32))
                        for i in range(2)]
                ps_s = es_p2.enter_context(nc.psum_tensor("ps_s", [128, 128], F32))

                dma("sync", sq, sel_sb[:], sel_in,
                    waits=[(sv, sv.n), (sp, sp.n), (sa, sa.n)])

                for (rsrc, dst) in ((gam_in, gam_rep), (bet_in, bet_rep)):
                    dma("sync", sq, xr[0][0:1, :], rsrc, waits=[(sp, sp.n)])
                    for piece in range(8):
                        emit("tensor", sp, lambda e, piece=piece: e.matmul(
                            ps_s[:, :], ones_f[0:1, :],
                            xr[0][0:1, piece * 128:(piece + 1) * 128],
                            start=True, stop=True, skip_group_check=True),
                            waits=[(sq, sq.n), (sv, sv.n)])
                        emit("vector", sv, lambda e, dst=dst, piece=piece: e.tensor_copy(
                            dst[:, piece * 128:(piece + 1) * 128], ps_s[:, :]),
                            waits=[(sp, sp.n)])

                emit("vector", sv, lambda e: e.memset(asum[:], 0.0))
                for c in range(8):
                    dma("sync", sq, xr[0][:], wffn[c * 128:(c + 1) * 128, :],
                        waits=[(sv, sv.n), (sp, sp.n)])
                    emit("vector", sv, lambda e: e.tensor_reduce(
                        wabs[:], xr[0][:], AX.X, ALU.add, apply_absolute_value=True),
                        waits=[(sq, sq.n)])
                    emit("vector", sv, lambda e: e.tensor_tensor(
                        asum[:], asum[:], wabs[:], ALU.add), waits=[(sv, sv.n)])
                emit("tensor", sp, lambda e: e.transpose(ps_s[0:1, 0:128], asum[:], ident_f[:]),
                     waits=[(sv, sv.n)])
                emit("vector", sv, lambda e: e.tensor_reduce(
                    sum_s[:], ps_s[0:1, 0:128], AX.X, ALU.add), waits=[(sp, sp.n)])
                emit("vector", sv, lambda e: e.tensor_scalar(
                    beta_s[:], sum_s[:], 1.0 / float(D * D), None, ALU.mult),
                    waits=[(sv, sv.n)])
                emit("vector", sv, lambda e: e.tensor_scalar(
                    rbeta_s[:], beta_s[:], 1e-5, None, ALU.add), waits=[(sv, sv.n)])
                emit("vector", sv, lambda e: e.reciprocal(rbeta_s[:], rbeta_s[:]),
                     waits=[(sv, sv.n)])
                for (src, dst) in ((beta_s, bcol), (rbeta_s, rbcol)):
                    emit("tensor", sp, lambda e, src=src: e.matmul(
                        ps_s[:, 0:1], ones_f[0:1, :], src[0:1, :],
                        start=True, stop=True, skip_group_check=True),
                        waits=[(sv, sv.n)])
                    emit("vector", sv, lambda e, dst=dst: e.tensor_copy(
                        dst[:], ps_s[:, 0:1]), waits=[(sp, sp.n)])
                for c in range(8):
                    dma("sync", sq, xr[0][:], wffn[c * 128:(c + 1) * 128, :],
                        waits=[(sv, sv.n)])
                    emit("vector", None, lambda e: e.tensor_scalar(
                        wqa[:], xr[0][:], rbcol[:], 0.5, ALU.mult, ALU.is_ge),
                        waits=[(sq, sq.n)])
                    emit("vector", None, lambda e: e.tensor_scalar(
                        wqb2[:], xr[0][:], rbcol[:], -0.5, ALU.mult, ALU.is_le))
                    emit("vector", None, lambda e: e.tensor_tensor(
                        wqa[:], wqa[:], wqb2[:], ALU.subtract))
                    emit("vector", sv, lambda e, c=c: e.tensor_scalar(
                        wq_bf[c][:], wqa[:], bcol[:], None, ALU.mult), waits=[(sv, sv.n)])
                if DEBUG_TAPS:
                    dma("sync", sq, dbg["dbg_wq0"], wq_bf[0][:], waits=[(sv, sv.n)])

                resid_done = {}
                x2w = {}
                scr_w = {}
                qs_used = {}
                poolb = {}
                act_done = {}
                aff_done = {}

                def stage_b(k):
                    emit("vector", sv, lambda e, k=k: e.tensor_copy(
                        c_col[k % 2][:], c_b[k % 2][:]), waits=[(sa, act_done[k])])
                    uw = [(sv, sv.n)]
                    if (k - 2) in poolb:
                        uw.append((sg, poolb[k - 2]))
                    emit("vector", sv, lambda e, k=k: e.tensor_scalar(
                        usb[k % 2][:], x2sb[k % 2][:], mv2[k % 2][:, 0:1],
                        c_col[k % 2][:], ALU.subtract, ALU.mult), waits=uw)
                    aff_done[k] = sv.n
                    emit("gpsimd", sg, lambda e, k=k: e.tensor_tensor(
                        usb[k % 2][:], usb[k % 2][:], gam_rep[:], ALU.mult),
                        waits=[(sv, sv.n)])
                    emit("gpsimd", sg, lambda e, k=k: e.tensor_tensor(
                        xn2[:, k * 1024:(k + 1) * 1024], usb[k % 2][:],
                        bet_rep[:], ALU.add))
                    poolb[k] = sg.n
                    if k > 0:
                        emit("vector", sv, lambda e, k=k: e.reduce_max(
                            rmx[:], xn2[:, (k - 1) * 1024: k * 1024], axis=AX.X,
                            apply_absolute_value=True), waits=[(sg, poolb[k - 1])])
                        emit("vector", sv, lambda e: e.tensor_tensor(
                            gmax[:], gmax[:], rmx[:], ALU.max), waits=[(sv, sv.n)])

                for g0 in range(2):
                    dma("sync", sxr[g0], xr[g0][:], x_in[g0 * 128:(g0 + 1) * 128, :],
                        waits=[(sv, sv.n)])
                    x2w[g0] = (sxr[g0], sxr[g0].n)
                for g in range(8):
                    def dmm(e, g=g):
                        ins = None
                        for c in range(8):
                            ins = e.matmul(ps_d[:, :], Kb[:, c * 16:(c + 1) * 16],
                                           qf[c][:, g * 512:(g + 1) * 512],
                                           start=(c == 0), stop=(c == 7),
                                           skip_group_check=True)
                        return ins
                    emit("tensor", sp, dmm, waits=[(sv, kb_ready), (sv, sv.n)])
                    emit("vector", sv, lambda e: e.tensor_scalar(
                        dh[:], ps_d[:, :], 1e-6, None, ALU.max), waits=[(sp, sp.n)])
                    emit("vector", sv, lambda e: e.reciprocal(dh[:], dh[:]),
                         waits=[(sv, sv.n)])
                    emit("vector", sv, lambda e: e.tensor_copy(recb[:], dh[:]),
                         waits=[(sv, sv.n)])
                    rec_ready = sv.n
                    if DEBUG_TAPS and g == 0:
                        dma("sync", sq, dbg["dbg_rec"], dh[:], waits=[(sv, rec_ready)])
                    for c in range(8):
                        bw = [(sv, rec_ready)]
                        if (g, c - 2) in qs_used:
                            bw.append((sv, qs_used[(g, c - 2)]))
                        emit("tensor", sp, lambda e, c=c: e.matmul(
                            ps_r[c % 2][:, :], sel_sb[:, c * 128:(c + 1) * 128], recb[:],
                            start=True, stop=True, skip_group_check=True), waits=bw)
                        emit("vector", sv, lambda e, c=c, g=g: e.tensor_tensor(
                            qs[:, c * 512:(c + 1) * 512],
                            qf[c][:, g * 512:(g + 1) * 512],
                            ps_r[c % 2][:, :], ALU.mult), waits=[(sp, sp.n)], fast=True)
                        qs_used[(g, c)] = sv.n
                    qs_ready = sv.n
                    if DEBUG_TAPS and g == 0:
                        dma("sync", sq, dbg["dbg_qs"], qs[:], waits=[(sv, qs_ready)])
                    for tt in range(4):
                        gg = 4 * g + tt

                        def pmm(e, tt=tt):
                            ins = None
                            for c in range(8):
                                for nh in range(2):
                                    ins = e.matmul(
                                        ps_p[tt % 2][:, nh * 512:(nh + 1) * 512],
                                        qs[:, c * 512 + tt * 128: c * 512 + (tt + 1) * 128],
                                        M_sb[c][:, nh * 512:(nh + 1) * 512],
                                        start=(c == 0), stop=(c == 7),
                                        skip_group_check=True)
                            return ins
                        pw = [(sv, qs_ready)]
                        if (gg - 2) in resid_done:
                            pw.append((sv, resid_done[gg - 2]))
                        emit("tensor", sp, pmm, waits=pw)
                        rw = [(sp, sp.n), x2w[gg]]
                        if (gg - 2) in scr_w:
                            rw.append(scr_w[gg - 2])
                        if (gg - 2) in aff_done:
                            rw.append((sv, aff_done[gg - 2]))
                        emit("vector", sv, lambda e, tt=tt, gg=gg: e.tensor_tensor(
                            x2sb[gg % 2][:], ps_p[tt % 2][:, :], xr[gg % 2][:], ALU.add),
                            waits=rw, fast=True)
                        resid_done[gg] = sv.n
                        if gg + 2 < NT:
                            dma("sync", sxr[gg % 2], xr[(gg + 2) % 2][:],
                                x_in[(gg + 2) * 128:(gg + 3) * 128, :],
                                waits=[(sv, resid_done[gg])])
                            x2w[gg + 2] = (sxr[gg % 2], sxr[gg % 2].n)
                        if DEBUG_TAPS and gg == 4:
                            dma("sync", sq, dbg["dbg_x2"], x2sb[0][:],
                                waits=[(sv, resid_done[4])])
                        dma("gpsimd", gw[gg % 2], x2_scr[gg * 128:(gg + 1) * 128, :],
                            x2sb[gg % 2][:], waits=[(sv, resid_done[gg])])
                        scr_w[gg] = (gw[gg % 2], gw[gg % 2].n)
                        emit("vector", sv, lambda e, gg=gg: (
                            e.bn_stats(st12b[:, 0:6], x2sb[gg % 2][:, 0:512]),
                            e.bn_stats(st12b[:, 6:12], x2sb[gg % 2][:, 512:1024]))[-1])
                        emit("vector", sv, lambda e, gg=gg: e.bn_aggr(
                            mv2[gg % 2][:], st12b[:]), waits=[(sv, sv.n)])
                        emit("vector", sv, lambda e, gg=gg: e.tensor_scalar(
                            veps2[gg % 2][:], mv2[gg % 2][:, 1:2], 1.001, 1e-8,
                            ALU.mult, ALU.add), waits=[(sv, sv.n)])
                        emit("scalar", sa, lambda e, gg=gg: e.activation(
                            c_a[:], veps2[gg % 2][:], AF.Ln), waits=[(sv, sv.n)])
                        emit("scalar", sa,
                             lambda e, gg=gg: e.activation(c_b[gg % 2][:], c_a[:],
                                                           AF.Exp, scale=-0.5),
                             waits=[(sa, sa.n)])
                        act_done[gg] = sa.n
                        if gg > 0:
                            stage_b(gg - 1)

                stage_b(31)
                emit("vector", sv, lambda e: e.reduce_max(
                    rmx[:], xn2[:, 31 * 1024: 32 * 1024], axis=AX.X,
                    apply_absolute_value=True), waits=[(sg, poolb[31])])
                emit("vector", sv, lambda e: e.tensor_tensor(
                    gmax[:], gmax[:], rmx[:], ALU.max), waits=[(sv, sv.n)])
                emit("tensor", sp, lambda e: e.transpose(ps_s[0:1, 0:128], gmax[:],
                                                         ident_f[:]), waits=[(sv, sv.n)])
                emit("vector", sv, lambda e: e.reduce_max(
                    gm_s[:], ps_s[0:1, 0:128], axis=AX.X), waits=[(sp, sp.n)])
                emit("vector", sv, lambda e: e.memset(dh[0:1, :], 0.0), waits=[(sv, sv.n)])
                emit("vector", sv, lambda e: e.tensor_copy(dh[0:1, 0:1], gm_s[:]),
                     waits=[(sv, sv.n)])
                dma("sync", sq, ccoll_i.ap(), dh[0:1, :], waits=[(sv, sv.n)])
                emit("gpsimd", None, lambda e: e.collective_compute(
                    "AllReduce", ALU.max, replica_groups=[list(range(NCORE))],
                    ins=[ccoll_i.ap().opt()], outs=[ccoll_o.ap().opt()]),
                    waits=[(sq, sq.n)])
                dma("sync", sq, dh[0:1, 0:512], ccoll_o.ap(), waits=[(sg, sg.n)])
                emit("vector", sv, lambda e: e.tensor_scalar(
                    gm_s[:], dh[0:1, 0:1], CLIPF, None, ALU.mult), waits=[(sq, sq.n)])
                emit("tensor", sp, lambda e: e.matmul(
                    ps_s[:, 0:1], ones_f[0:1, :], gm_s[0:1, :], start=True, stop=True,
                    skip_group_check=True), waits=[(sv, sv.n)])
                emit("vector", sv, lambda e: e.tensor_copy(gcol[:], ps_s[:, 0:1]),
                     waits=[(sp, sp.n)])
                emit("vector", sv, lambda e: e.tensor_scalar(
                    gneg[:], gcol[:], -1.0, None, ALU.mult), waits=[(sv, sv.n)])
                if DEBUG_TAPS:
                    dma("sync", sq, dbg["dbg_xn2"], xn2[:, 4 * 1024:5 * 1024], waits=[(sv, sv.n)])
                    dma("sync", sq, dbg["dbg_xn2all"], xn2[:, :], waits=[(sv, sv.n)])
                    dma("sync", sq, dbg["dbg_gm"], gm_s[:], waits=[(sv, sv.n)])
                p2_scr_all = (gw[0].n, gw[1].n)

            with contextlib.ExitStack() as es_p3:
                pass_marker = None
                def sb3(name, shape, dt):
                    return es_p3.enter_context(nc.sbuf_tensor(name, shape, dt))
                x2r = [sb3(f"x2r{i}", [128, D], F32) for i in range(2)]
                xq_tm = [sb3(f"xq_tm{i}", [128, D], BF16) for i in range(2)]
                xq_fm = [sb3(f"xqfm{i}", [128, D], BF16) for i in range(2)]
                out_sb = [sb3(f"outsb{i}", [128, D], F32) for i in range(2)]
                ps_T3 = [es_p3.enter_context(nc.psum_tensor(f"ps_T3{i}", [128, 1024], BF16))
                         for i in range(2)]
                ps_f = [es_p3.enter_context(nc.psum_tensor(f"ps_f{i}", [128, 1024], F32))
                        for i in range(2)]

                x2r_w = {}
                T3_mm = {}
                out_done = {}
                y_w = {}
                b4p3 = {}
                for g0 in range(2):
                    dma("sync", sx2r[g0], x2r[g0][:], x2_scr[g0 * 128:(g0 + 1) * 128, :],
                        waits=[(gw[0], p2_scr_all[0]), (gw[1], p2_scr_all[1])])
                    x2r_w[g0] = (sx2r[g0], sx2r[g0].n)
                def emit_clipT3(gg):
                    cw = [(sv, sv.n)]
                    if gg == 0:
                        cw += [(gw[0], gw[0].n), (gw[1], gw[1].n)]
                    if (gg - 2) in T3_mm:
                        cw.append((sp, T3_mm[gg - 2]))
                    emit("vector", sv, lambda e, gg=gg: e.tensor_scalar(
                        xq_tm[gg % 2][:], xn2[:, gg * 1024:(gg + 1) * 1024], gcol[:],
                        gneg[:], ALU.min, ALU.max), waits=cw, fast=True)
                    xq_ready = sv.n

                    def tr3(e, gg=gg):
                        ins = None
                        for c in range(8):
                            ins = e.transpose(ps_T3[gg % 2][:, c * 128:(c + 1) * 128],
                                              xq_tm[gg % 2][:, c * 128:(c + 1) * 128],
                                              ident_b[:])
                        return ins
                    emit("tensor", sp, tr3,
                         waits=[(sv, xq_ready)] + ([(sv, b4p3[gg % 2])] if (gg % 2) in b4p3 else []))
                    T3_mm[gg] = sp.n
                    emit("vector", sv, lambda e, gg=gg: e.tensor_copy(
                        xq_fm[gg % 2][:], ps_T3[gg % 2][:, :]), waits=[(sp, T3_mm[gg])],
                        fast=True)
                    b4p3[gg % 2] = sv.n
                    fm_done3[gg] = sv.n

                def emit_ffn(gg):
                    def fmm(e, gg=gg):
                        ins = None
                        for c in range(8):
                            for nh in range(2):
                                ins = e.matmul(
                                    ps_f[gg % 2][:, nh * 512:(nh + 1) * 512],
                                    xq_fm[gg % 2][:, c * 128:(c + 1) * 128],
                                    wq_bf[c][:, nh * 512:(nh + 1) * 512],
                                    start=(c == 0), stop=(c == 7),
                                    skip_group_check=True)
                        return ins
                    fw = [(sv, fm_done3[gg])]
                    if (gg - 2) in out_done:
                        fw.append((sv, out_done[gg - 2]))
                    emit("tensor", sp, fmm, waits=fw)
                    ow = [(sp, sp.n), x2r_w[gg]]
                    if (gg - 2) in y_w:
                        ow.append(y_w[gg - 2])
                    emit("vector", sv, lambda e, gg=gg: e.tensor_tensor(
                        out_sb[gg % 2][:], ps_f[gg % 2][:, :], x2r[gg % 2][:], ALU.add),
                        waits=ow, fast=True)
                    out_done[gg] = sv.n
                    dma("gpsimd", gy[gg % 2], y_out[gg * 128:(gg + 1) * 128, :],
                        out_sb[gg % 2][:], waits=[(sv, out_done[gg])])
                    y_w[gg] = (gy[gg % 2], gy[gg % 2].n)
                    if gg + 2 < NT:
                        dma("sync", sx2r[gg % 2], x2r[(gg + 2) % 2][:],
                            x2_scr[(gg + 2) * 128:(gg + 3) * 128, :],
                            waits=[(sv, out_done[gg])])
                        x2r_w[gg + 2] = (sx2r[gg % 2], sx2r[gg % 2].n)

                fm_done3 = {}
                emit_clipT3(0)
                for gg in range(NT):
                    if gg + 1 < NT:
                        emit_clipT3(gg + 1)
                    emit_ffn(gg)

                def fin(e):
                    e.wait_ge(gy[0].h, gy[0].n)
                    e.wait_ge(gy[1].h, gy[1].n)
                on("sync", fin)
            es_x.close()

    return nc


def kernel(L0_attn_x, L0_attn_w_qkv, L0_attn_w_out, L0_ffn_weight,
           L0_ffn_ln_gamma, L0_ffn_ln_beta):
    x = np.ascontiguousarray(np.asarray(L0_attn_x, dtype=np.float32))
    w_qkv = np.asarray(L0_attn_w_qkv, dtype=np.float32)
    w_out = np.asarray(L0_attn_w_out, dtype=np.float32)
    W = np.ascontiguousarray(np.asarray(L0_ffn_weight, dtype=np.float32))
    gam = np.asarray(L0_ffn_ln_gamma, dtype=np.float32).reshape(1, D)
    bet = np.asarray(L0_ffn_ln_beta, dtype=np.float32).reshape(1, D)

    cc_b, ss_b, perm, sel, mask2, ident_b, ident_f = _host_consts()

    wqkvT = np.empty((D, 3 * D), dtype=np.float32)
    wqkvT[:, 0:D] = w_qkv[0:D][perm].T
    wqkvT[:, D:2 * D] = w_qkv[D:2 * D][perm].T
    wqkvT[:, 2 * D:3 * D] = w_qkv[2 * D:3 * D].T
    wqkvT = np.ascontiguousarray(wqkvT)
    woutT = np.ascontiguousarray(w_out.T)

    nc = build_nc()
    shared = {
        "wqkvT": wqkvT, "woutT": woutT, "wffn": W,
        "gam_in": np.ascontiguousarray(gam), "bet_in": np.ascontiguousarray(bet),
        "cc_in": cc_b, "ss_in": ss_b, "sel_in": sel, "mask_in": mask2,
        "idb_in": ident_b, "idf_in": ident_f,
    }
    in_maps = []
    for b in range(NCORE):
        m = dict(shared)
        m["x_in"] = np.ascontiguousarray(x[b])
        in_maps.append(m)
    res = run_bass_kernel_spmd(nc, in_maps, core_ids=list(range(NCORE)))
    return np.stack([res.results[b]["y_out"] for b in range(NCORE)], axis=0)


# version 20 (final)
# speedup vs baseline: 1.2001x; 1.2001x over previous
import contextlib
import numpy as np
import ml_dtypes

import concourse.bass as bass
import concourse.mybir as mybir
from concourse.bass_utils import run_bass_kernel_spmd

F32 = mybir.dt.float32
BF16 = mybir.dt.bfloat16
AF = mybir.ActivationFunctionType
ALU = mybir.AluOpType
AX = mybir.AxisListType
BF = ml_dtypes.bfloat16

T = 4096
D = 1024
H = 16
HD = 64
NT = 32
NS = 8
NCORE = 8
DEBUG_TAPS = False
STRICT_EDGES = False
CLIPF = float(np.float32(np.float32(128.0 - 1e-5) / np.float32(128.0)))


class Ctr:

    def __init__(self, es, nc, name):
        self.h = es.enter_context(nc.semaphore(name))
        self.n = 0

    def bump(self, k=1):
        self.n += k
        return self.n


def _host_consts():
    inv_freq = (1.0 / np.float32(10000.0) ** (
        np.arange(0, HD, 2, dtype=np.float32) / np.float32(HD))).astype(np.float32)
    t = np.arange(T, dtype=np.float32)
    freqs = (t[:, None] * inv_freq[None, :]).astype(np.float32)
    cosv = np.cos(freqs).astype(np.float32)
    sinv = np.sin(freqs).astype(np.float32)
    cc = np.empty((128, T), dtype=np.float32)
    ss = np.empty((128, T), dtype=np.float32)
    for j in range(128):
        f = j % 32
        cc[j] = cosv[:, f]
        ss[j] = sinv[:, f] if j < 64 else (-sinv[:, f])
    perm = np.empty(D, dtype=np.int64)
    for c in range(8):
        for j in range(128):
            sub = (j % 64) // 32
            half = j // 64
            f = j % 32
            perm[c * 128 + j] = (2 * c + sub) * HD + half * 32 + f
    sel = np.zeros((16, D), dtype=BF)
    for c in range(8):
        for m in range(128):
            h = 2 * c + (m // 32) % 2
            sel[h, c * 128 + m] = 1.0
    mask2 = np.zeros((128, 2), dtype=np.float32)
    for p in range(128):
        mask2[p, 0] = 1.0 if (p // 32) % 2 == 0 else 0.0
        mask2[p, 1] = 1.0 - mask2[p, 0]
    ident_b = np.eye(128, dtype=BF)
    ident_f = np.eye(128, dtype=np.float32)
    return cc.astype(BF), ss.astype(BF), perm, sel, mask2, ident_b, ident_f


def build_nc():
    nc = bass.Bass("TRN2", target_bir_lowering=False)

    x_in = nc.dram_tensor("x_in", [T, D], F32, kind="ExternalInput").ap()
    wqkvT = nc.dram_tensor("wqkvT", [D, 3 * D], F32, kind="ExternalInput").ap()
    woutT = nc.dram_tensor("woutT", [D, D], F32, kind="ExternalInput").ap()
    wffn = nc.dram_tensor("wffn", [D, D], F32, kind="ExternalInput").ap()
    gam_in = nc.dram_tensor("gam_in", [1, D], F32, kind="ExternalInput").ap()
    bet_in = nc.dram_tensor("bet_in", [1, D], F32, kind="ExternalInput").ap()
    cc_in = nc.dram_tensor("cc_in", [128, T], BF16, kind="ExternalInput").ap()
    ss_in = nc.dram_tensor("ss_in", [128, T], BF16, kind="ExternalInput").ap()
    sel_in = nc.dram_tensor("sel_in", [16, D], BF16, kind="ExternalInput").ap()
    mask_in = nc.dram_tensor("mask_in", [128, 2], F32, kind="ExternalInput").ap()
    idb_in = nc.dram_tensor("idb_in", [128, 128], BF16, kind="ExternalInput").ap()
    idf_in = nc.dram_tensor("idf_in", [128, 128], F32, kind="ExternalInput").ap()
    y_out = nc.dram_tensor("y_out", [T, D], F32, kind="ExternalOutput").ap()
    dbg = {}
    if DEBUG_TAPS:
        for nm, shape, dt in [
            ("dbg_xn1", [128, D], BF16), ("dbg_z00", [128, 512], BF16),
            ("dbg_qf0", [128, T], BF16), ("dbg_kv", [64, D], BF16),
            ("dbg_ks", [128, 8], F32), ("dbg_Kb", [128, 128], BF16),
            ("dbg_M0", [128, D], BF16), ("dbg_rec", [16, 512], F32),
            ("dbg_x2", [128, D], F32), ("dbg_xn2", [128, D], BF16),
            ("dbg_gm", [1, 1], F32), ("dbg_ktm", [128, 4096], BF16),
            ("dbg_qs", [128, 4096], BF16), ("dbg_xr", [128, D], F32),
            ("dbg_wq0", [128, D], BF16), ("dbg_xqfm", [128, D], BF16),
            ("dbg_xn2all", [128, 32768], BF16),
            ("dbg_vtm", [128, 4096], BF16), ("dbg_fm", [128, 4096], BF16),
        ]:
            dbg[nm] = nc.dram_tensor(nm, shape, dt, kind="ExternalOutput").ap()
    x2_scr = nc.dram_tensor("x2_scr", [T, D], F32).ap()
    ccoll_i = nc.dram_tensor("ccoll_i", [1, 512], F32)
    ccoll_o = nc.dram_tensor("ccoll_o", [1, 512], F32)

    es = contextlib.ExitStack()
    with es:
        sq = Ctr(es, nc, "sq")
        gq = Ctr(es, nc, "gq")
        scc = Ctr(es, nc, "scc")
        sv = Ctr(es, nc, "sv")
        sa = Ctr(es, nc, "sa")
        sp = Ctr(es, nc, "sp")
        sg = Ctr(es, nc, "sg")
        sx = [Ctr(es, nc, f"sx{i}") for i in range(2)]
        stb = Ctr(es, nc, "stb")
        sxr = [Ctr(es, nc, f"sxr{i}") for i in range(2)]
        sx2r = [Ctr(es, nc, f"sx2r{i}") for i in range(2)]
        gw = [Ctr(es, nc, f"gw{i}") for i in range(2)]
        gy = [Ctr(es, nc, f"gy{i}") for i in range(2)]
        swm = Ctr(es, nc, "swm")
        swc = Ctr(es, nc, "swc")
        _eng_ctr_map = {}

        def sb(name, shape, dt):
            return es.enter_context(nc.sbuf_tensor(name, shape, dt))

        M_sb = [sb(f"M_sb{c}", [128, D], BF16) for c in range(8)]
        ident_b = sb("identb", [128, 128], BF16)
        ident_f = sb("identf", [128, 128], F32)
        ones_col = sb("ones_col", [128, 1], BF16)
        ones_f = sb("ones_f", [1, 128], F32)
        mask_sb = sb("mask2s", [128, 2], F32)
        Kb = sb("Kb", [128, 128], BF16)
        gmax = sb("gmax", [128, 1], F32)
        gm_s = sb("gm_s", [1, 1], F32)
        gcol = sb("gcol", [128, 1], F32)
        gneg = sb("gneg", [128, 1], F32)
        bcol = sb("bcol", [128, 1], F32)
        rbcol = sb("rbcol", [128, 1], F32)

        block = es.enter_context(nc.Block())
        ENG = {"sync": block.sync, "vector": block.vector, "scalar": block.scalar,
               "tensor": block.tensor, "gpsimd": block.gpsimd}

        def on(eng, fn):
            ENG[eng](fn)
        _eng_ctr_map.update({"vector": sv, "scalar": sa, "gpsimd": sg})

        def _pin_tables(e):
            e.add_instruction(mybir.InstLoadActFuncSet(
                name=nc.get_next_instruction_name(),
                act_func_set_id=6, ins=[], outs=[]))
        on("scalar", _pin_tables)

        def dma(eng, ctr, out, in_, waits=()):
            def body(e):
                for (c, v) in waits:
                    e.wait_ge(c.h, v)
                e.dma_start(out=out, in_=in_).then_inc(ctr.h, 16)
            on(eng, body)
            ctr.bump(16)

        ENG_CTR = _eng_ctr_map

        def emit(eng, ctr, fn, waits=(), inc=1, fast=False):
            self_ctr = ENG_CTR.get(eng)
            if self_ctr is not None:
                if ctr is None:
                    ctr = self_ctr
                assert ctr is self_ctr, "serializing engines must inc their own ctr"
                selfw = [] if (fast and not STRICT_EDGES) else [(self_ctr, self_ctr.n)]
            else:
                selfw = []

            def body(e):
                for (c, v) in selfw:
                    e.wait_ge(c.h, v)
                for (c, v) in waits:
                    e.wait_ge(c.h, v)
                ins = fn(e)
                if ctr is not None and ins is not None:
                    ins.then_inc(ctr.h, inc)
            on(eng, body)
            if ctr is not None:
                ctr.bump(inc)

        with contextlib.ExitStack() as es_qf:
            qf = [es_qf.enter_context(nc.sbuf_tensor(f"qf{c}", [128, T], BF16))
                  for c in range(8)]

            with contextlib.ExitStack() as es_p1:
                def sb1(name, shape, dt):
                    return es_p1.enter_context(nc.sbuf_tensor(name, shape, dt))

                wqkvT_bf = [sb1(f"wqkvTb{c}", [128, 3 * D], BF16) for c in range(8)]
                tab_cc = sb1("tab_cc", [128, 512], BF16)
                tab_ss = sb1("tab_ss", [128, 512], BF16)
                xbuf = [sb1(f"xbuf{i}", [128, D], F32) for i in range(2)]
                xdma_done = {}
                tabs = {}
                for g0 in range(2):
                    dma("sync", sx[g0], xbuf[g0][:], x_in[g0 * 128:(g0 + 1) * 128, :])
                    xdma_done[g0] = (sx[g0], sx[g0].n)
                dma("sync", stb, tab_cc[:], cc_in[:, 0:512])
                dma("sync", stb, tab_ss[:], ss_in[:, 0:512])
                tabs[0] = stb.n
                dma("sync", sq, ident_b[:], idb_in)
                dma("sync", sq, ident_f[:], idf_in)
                dma("sync", sq, mask_sb[:], mask_in)
                const_ready = sq.n
                emit("vector", sv, lambda e: e.memset(ones_col[:], 1.0))
                emit("vector", sv, lambda e: e.memset(ones_f[:], 1.0))
                emit("vector", sv, lambda e: e.memset(gmax[:], 0.0))
                emit("vector", sv, lambda e: e.memset(Kb[:], 0.0))
                for c in range(8):
                    dma("gpsimd", swc, wqkvT_bf[c][:, :], wqkvT[c * 128:(c + 1) * 128, :])
                for c in range(8):
                    emit("vector", sv, lambda e, c=c: e.tensor_scalar(
                        wqkvT_bf[c][:, 0:D], wqkvT_bf[c][:, 0:D], 0.125, None, ALU.mult),
                        waits=[(swc, swc.n)] if c == 0 else [])
                wqkv_ready_v = sv.n
                wqkv_ready = swc.n

                woutc = [sb1(f"woutc{i}", [64, 2 * D], BF16) for i in range(2)]
                xn1_tm = [sb1(f"xn1tm{i}", [128, D], BF16) for i in range(4)]
                xn1_fm = [sb1(f"xn1fm{i}", [128, 4096], BF16) for i in range(2)]
                ktm = sb1("ktm", [128, 4096], BF16)
                vtm = sb1("vtm", [128, 4096], BF16)
                zsrc = [sb1(f"zsrc{i}", [128, 512], BF16) for i in range(2)]
                t1b = [sb1(f"t1b{i}", [128, 512], BF16) for i in range(2)]
                t2b = [sb1(f"t2b{i}", [128, 512], BF16) for i in range(2)]
                zb = [sb1(f"zb{i}", [128, 512], BF16) for i in range(2)]
                kfm = [sb1(f"kfm{i}", [128, 512], BF16) for i in range(2)]
                kv_sb = sb1("kv_sb", [64, D], BF16)
                ks_sb = sb1("ks_sb", [128, 8], F32)
                st12 = sb1("st12", [128, 12], F32)
                mv = sb1("mv", [128, 2], F32)
                veps = sb1("veps", [128, 1], F32)
                rstd_a = sb1("rstd_a", [128, 1], F32)
                rstd_b = sb1("rstd_b", [128, 1], F32)
                rstd = sb1("rstd", [128, 1], F32)

                ps_qk = [es_p1.enter_context(nc.psum_tensor(f"psqk{i}", [128, 512], F32))
                         for i in range(2)]
                ps_v = [es_p1.enter_context(nc.psum_tensor(f"psv{i}", [128, 512], F32))
                        for i in range(2)]
                ps_T = es_p1.enter_context(nc.psum_tensor("psT", [128, 1024], BF16))
                ps_vk = es_p1.enter_context(nc.psum_tensor("psvk", [64, 1024], F32))
                ps_ks = es_p1.enter_context(nc.psum_tensor("psks", [128, 8], F32))

                xn1fm_view = [f.ap().rearrange("p (c t j) -> p c t j", c=8, t=4)
                              for f in xn1_fm]
                ktm_view = ktm.ap().rearrange("p (t c j) -> p t c j", t=4, c=8)
                ktm_head = ktm.ap().rearrange("p (t c h s f) -> p t c h s f",
                                              t=4, c=8, h=2, s=2)
                psT_view8 = ps_T.ap().rearrange("p (c j) -> p c j", c=8)
                psT_view4 = ps_T[:, 0:512].rearrange("p (t j) -> p t j", t=4)

                ln_done = {}
                fmev = {}
                b4_free = [sv.n]
                zev = {}
                zsrc_free = [sv.n, sv.n]
                vev = {}
                kT_mm = {}
                T_mm = {}
                ktmv = {}
                qk_mm = {}
                v_mm = {}
                last_rope_sv = [0]

                def emit_ln(g):
                    bi = g % 4
                    xi = g % 2
                    emit("vector", sv, lambda e, xi=xi: (
                        e.bn_stats(st12[:, 0:6], xbuf[xi][:, 0:512]),
                        e.bn_stats(st12[:, 6:12], xbuf[xi][:, 512:1024]))[-1],
                        waits=[xdma_done[g]])
                    emit("vector", sv, lambda e: e.bn_aggr(mv[:], st12[:]),
                         waits=[(sv, sv.n)])
                    emit("vector", sv, lambda e: e.tensor_scalar(
                        veps[:], mv[:, 1:2], 1e-5, None, ALU.add), waits=[(sv, sv.n)])
                    emit("scalar", sa, lambda e: e.activation(rstd_a[:], veps[:], AF.Ln),
                         waits=[(sv, sv.n)])
                    emit("scalar", sa,
                         lambda e: e.activation(rstd_b[:], rstd_a[:], AF.Exp, scale=-0.5),
                         waits=[(sa, sa.n)])
                    emit("vector", sv, lambda e: e.tensor_copy(rstd[:], rstd_b[:]),
                         waits=[(sa, sa.n)])
                    aff_w = [(sv, sv.n)]
                    if (g - 4) in T_mm:
                        aff_w.append((sp, T_mm[g - 4]))
                    emit("vector", sv, lambda e, bi=bi, xi=xi: e.tensor_scalar(
                        xn1_tm[bi][:], xbuf[xi][:], mv[:, 0:1], rstd[:],
                        ALU.subtract, ALU.mult), waits=aff_w)
                    ln_done[g] = sv.n
                    if g + 2 < NT:
                        dma("sync", sx[g % 2], xbuf[(g + 2) % 2][:],
                            x_in[(g + 2) * 128:(g + 3) * 128, :],
                            waits=[(sv, ln_done[g])])
                        xdma_done[g + 2] = (sx[g % 2], sx[g % 2].n)

                def emit_T(g):
                    bi = g % 4
                    def tr(e, bi=bi):
                        ins = None
                        for c in range(8):
                            ins = e.transpose(ps_T[:, c * 128:(c + 1) * 128],
                                              xn1_tm[bi][:, c * 128:(c + 1) * 128],
                                              ident_b[:])
                        return ins
                    tw = [(sv, ln_done[g]), (sv, b4_free[0])]
                    if g == 0:
                        tw.append((sq, const_ready))
                    emit("tensor", sp, tr, waits=tw)
                    T_mm[g] = sp.n
                    tt = g % 4
                    fb = (g // 4) % 2
                    emit("vector", sv, lambda e, tt=tt, fb=fb: e.tensor_copy(
                        xn1fm_view[fb][:, :, tt, :], psT_view8[:, :, :]),
                        waits=[(sp, T_mm[g])], fast=True)
                    fmev[g] = sv.n
                    b4_free[0] = sv.n

                emit_ln(0)
                if DEBUG_TAPS:
                    dma("sync", sq, dbg["dbg_xn1"], xn1_tm[0][:],
                        waits=[(sv, ln_done[0])])
                for g0 in (1, 2, 3):
                    emit_ln(g0)
                for g0 in (0, 1, 2, 3):
                    emit_T(g0)


                featv = {}
                for s in range(NS):
                    fm_ready = max(fmev[4 * s + t] for t in range(4))

                    def emit_v(tt, s=s):
                        g = 4 * s + tt
                        waits = [(sv, fm_ready), (swc, wqkv_ready)]
                        if (g - 1) in vev:
                            waits.append((sa, vev[g - 1]))

                        def vmm2(e, tt=tt, s=s):
                            ins = None
                            for half in range(2):
                                for kchunk in range(8):
                                    ins = e.matmul(
                                        ps_v[half][:, :],
                                        xn1_fm[s % 2][:, kchunk * 512 + tt * 128:
                                                      kchunk * 512 + (tt + 1) * 128],
                                        wqkvT_bf[kchunk][:, 2 * D + half * 512:
                                                         2 * D + (half + 1) * 512],
                                        start=(kchunk == 0), stop=(kchunk == 7),
                                        skip_group_check=True)
                            return ins
                        emit("tensor", sp, vmm2, waits=waits)
                        v_mm[g] = sp.n
                        emit("scalar", None, lambda e, tt=tt: e.copy(
                            vtm[:, tt * 1024: tt * 1024 + 512], ps_v[0][:, :]),
                            waits=[(sp, v_mm[g])], fast=True)
                        emit("scalar", sa, lambda e, tt=tt: e.copy(
                            vtm[:, tt * 1024 + 512: tt * 1024 + 1024], ps_v[1][:, :]),
                            fast=True)
                        vev[g] = sa.n

                    def emit_kT(c, s=s):
                        def ktr(e, c=c):
                            ins = None
                            for p in range(4):
                                ins = e.transpose(ps_T[:, p * 128:(p + 1) * 128],
                                                  kfm[c % 2][:, p * 128:(p + 1) * 128],
                                                  ident_b[:])
                            return ins
                        emit("tensor", sp, ktr,
                             waits=[(sv, featv[(s, c + 8)]), (sv, b4_free[0])])
                        kT_mm[(s, c)] = sp.n
                        emit("vector", sv, lambda e, c=c: e.tensor_copy(
                            ktm_view[:, :, c, :], psT_view4[:, :, :]),
                            waits=[(sp, kT_mm[(s, c)])], fast=True)
                        ktmv[(s, c)] = sv.n
                        b4_free[0] = sv.n

                    for kc in range(16):
                        if kc % 4 == 3 and s < NS - 1:
                            emit_ln(4 * (s + 1) + kc // 4)
                        cB = kc % 2
                        c = kc if kc < 8 else kc - 8
                        boff = 0 if kc < 8 else D
                        prev = (s, kc - 2) if kc >= 2 else \
                               ((s - 1, 14 + kc) if s > 0 else None)
                        waits = [(sv, fm_ready)]
                        if s == 0 and kc == 0:
                            waits.append((swc, wqkv_ready))
                            waits.append((sv, wqkv_ready_v))
                        if prev is not None:
                            waits.append((sa, zev[prev]))

                        def qkmm(e, cB=cB, c=c, boff=boff, s=s):
                            ins = None
                            for kchunk in range(8):
                                ins = e.matmul(
                                    ps_qk[cB][:, :],
                                    wqkvT_bf[kchunk][:, boff + c * 128: boff + (c + 1) * 128],
                                    xn1_fm[s % 2][:, kchunk * 512:(kchunk + 1) * 512],
                                    start=(kchunk == 0), stop=(kchunk == 7),
                                    skip_group_check=True)
                            return ins
                        emit("tensor", sp, qkmm, waits=waits)
                        qk_mm[(s, kc)] = sp.n
                        if kc >= 10:
                            emit_kT(kc - 10)
                        if kc % 4 == 2:
                            emit_v(kc // 4)
                        if kc % 4 == 2 and kc >= 6 and s < NS - 1:
                            emit_T(4 * (s + 1) + kc // 4 - 1)
                        pz = kc % 2
                        emit("scalar", sa, lambda e, cB=cB, pz=pz: e.copy(
                            zsrc[pz][:], ps_qk[cB][:, :]),
                            waits=[(sp, qk_mm[(s, kc)]), (sv, zsrc_free[pz])], fast=True)
                        zev[(s, kc)] = sa.n
                        emit("vector", None,
                             lambda e, pz=pz: e.tensor_tensor(t1b[pz][:], zsrc[pz][:],
                                                              tab_cc[:], ALU.mult),
                             waits=[(sa, zev[(s, kc)]), (stb, tabs[s])], fast=True)
                        emit("vector", None, lambda e, pz=pz: e.tensor_tensor(
                            t2b[pz][0:64, :], zsrc[pz][64:128, :], tab_ss[64:128, :],
                            ALU.mult), fast=True)
                        emit("vector", sv, lambda e, pz=pz: e.tensor_tensor(
                            t2b[pz][64:128, :], zsrc[pz][0:64, :], tab_ss[0:64, :],
                            ALU.mult), fast=True)
                        zsrc_free[pz] = sv.n
                        emit("vector", sv,
                             lambda e, pz=pz: e.tensor_tensor(zb[pz][:], t1b[pz][:],
                                                              t2b[pz][:], ALU.add),
                             fast=True)
                        emit("vector", sv, lambda e, pz=pz: e.tensor_scalar(
                            t1b[pz][:], zb[pz][:], 0.0, None, ALU.min), fast=True)
                        last_rope_sv[0] = sv.n
                        emit("scalar", sa, lambda e, pz=pz: e.activation(
                            t2b[pz][:], t1b[pz][:], AF.Exp), waits=[(sv, sv.n)], fast=True)
                        dest = qf[c][:, s * 512:(s + 1) * 512] if kc < 8 else kfm[c % 2][:, :]
                        kt_wait = [(sa, sa.n)]
                        if kc >= 8:
                            if (s, c - 2) in kT_mm:
                                kt_wait.append((sp, kT_mm[(s, c - 2)]))
                            elif (s - 1, 6 + c) in kT_mm:
                                kt_wait.append((sp, kT_mm[(s - 1, 6 + c)]))
                        emit("vector", sv, lambda e, dest=dest, pz=pz: e.scalar_tensor_tensor(
                            dest, zb[pz][:], 0.0, t2b[pz][:], ALU.max, ALU.add),
                            waits=kt_wait, fast=True)
                        featv[(s, kc)] = sv.n
                    if s < NS - 1:
                        dma("sync", stb, tab_cc[:], cc_in[:, (s + 1) * 512:(s + 2) * 512],
                            waits=[(sv, last_rope_sv[0])])
                        dma("sync", stb, tab_ss[:], ss_in[:, (s + 1) * 512:(s + 2) * 512])
                        tabs[s + 1] = stb.n
                    emit_kT(6)
                    emit_kT(7)
                    if s < NS - 1:
                        emit_T(4 * (s + 1) + 3)
                    if DEBUG_TAPS and s == 0:
                        dma("sync", sq, dbg["dbg_ktm"], ktm[:], waits=[(sv, max(
                            ktmv[(0, cc0)] for cc0 in range(8)))])
                        dma("sync", sq, dbg["dbg_vtm"], vtm[:], waits=[(sa, vev[3])])
                    ktm_ready = max(ktmv[(s, c)] for c in range(8))
                    for tt in range(4):
                        g = 4 * s + tt

                        def vkmm(e, tt=tt, s=s):
                            ins = None
                            for h in range(H):
                                c, sub = h // 2, h % 2
                                ins = e.matmul(
                                    ps_vk[0:64, h * 64:(h + 1) * 64],
                                    vtm[:, tt * 1024 + h * 64: tt * 1024 + (h + 1) * 64],
                                    ktm_head[:, tt, c, :, sub, :],
                                    start=(s == 0 and tt == 0 and h % 8 == 0),
                                    stop=(s == NS - 1 and tt == 3),
                                    skip_group_check=True)
                            for c in range(8):
                                ins = e.matmul(
                                    ps_ks[:, c:c + 1],
                                    ktm[:, tt * 1024 + c * 128: tt * 1024 + (c + 1) * 128],
                                    ones_col[:],
                                    start=(s == 0 and tt == 0 and c == 0),
                                    stop=(s == NS - 1 and tt == 3),
                                    skip_group_check=True)
                            return ins
                        emit("tensor", sp, vkmm, waits=[(sa, vev[g]), (sv, ktm_ready)])

                vk_done = sp.n
                emit("vector", sv, lambda e: e.tensor_copy(kv_sb[:], ps_vk[0:64, :]),
                     waits=[(sp, vk_done)])
                emit("vector", sv, lambda e: e.tensor_copy(ks_sb[:], ps_ks[:, :]))
                ks_ready = sv.n
                for c in range(8):
                    emit("vector", sv, lambda e, c=c: e.tensor_scalar(
                        Kb[:, c * 16 + 2 * c: c * 16 + 2 * c + 2], mask_sb[:, :],
                        ks_sb[:, c:c + 1], None, ALU.mult),
                        waits=[(sv, ks_ready), (sq, const_ready)])
                kb_ready = sv.n
                kv_ready = kb_ready

                wout_dma = {}
                mm_done_c = {}
                for c in range(2):
                    dma("gpsimd", swm, woutc[c % 2][0:64, 0:D],
                        woutT[c * 128: c * 128 + 64, :])
                    dma("gpsimd", swm, woutc[c % 2][0:64, D:2 * D],
                        woutT[c * 128 + 64: c * 128 + 128, :])
                    wout_dma[c] = swm.n
                for c in range(8):
                    if c + 2 < 8:
                        w = [(sp, mm_done_c[c])] if c in mm_done_c else []
                    h1, h2 = 2 * c, 2 * c + 1

                    def mmm(e, h1=h1, h2=h2, c=c):
                        ins = None
                        for nh in range(2):
                            pv = ps_v[nh]
                            pq = ps_qk[nh]
                            N1 = slice(nh * 512, (nh + 1) * 512)
                            N2 = slice(D + nh * 512, D + (nh + 1) * 512)
                            wc = woutc[c % 2]
                            ins = e.matmul(pv[0:32, :], kv_sb[0:64, h1 * 64: h1 * 64 + 32],
                                           wc[0:64, N1], start=True, stop=True,
                                           skip_group_check=True)
                            ins = e.matmul(pv[32:64, :], kv_sb[0:64, h2 * 64: h2 * 64 + 32],
                                           wc[0:64, N2], start=True, stop=True,
                                           skip_group_check=True)
                            ins = e.matmul(pv[64:96, :], kv_sb[0:64, h1 * 64 + 32: h1 * 64 + 64],
                                           wc[0:64, N1], start=True, stop=True,
                                           skip_group_check=True)
                            ins = e.matmul(pq[0:32, :], kv_sb[0:64, h2 * 64 + 32: h2 * 64 + 64],
                                           wc[0:64, N2], start=True, stop=True,
                                           skip_group_check=True)
                        return ins
                    emit("tensor", sp, mmm,
                         waits=[(swm, swm.n), (sv, sv.n)])
                    mdone = sp.n
                    mm_done_c[c] = sp.n
                    if c + 2 < 8:
                        dma("gpsimd", swm, woutc[c % 2][0:64, 0:D],
                            woutT[(c + 2) * 128: (c + 2) * 128 + 64, :],
                            waits=[(sp, mm_done_c[c])])
                        dma("gpsimd", swm, woutc[c % 2][0:64, D:2 * D],
                            woutT[(c + 2) * 128 + 64: (c + 2) * 128 + 128, :])
                        wout_dma[c + 2] = swm.n
                    emit("vector", None, lambda e, c=c: e.tensor_copy(
                        M_sb[c][0:96, 0:512], ps_v[0][0:96, :]), waits=[(sp, mdone)])
                    emit("vector", None, lambda e, c=c: e.tensor_copy(
                        M_sb[c][0:96, 512:1024], ps_v[1][0:96, :]))
                    emit("vector", None, lambda e, c=c: e.tensor_copy(
                        M_sb[c][96:128, 0:512], ps_qk[0][0:32, :]))
                    emit("vector", sv, lambda e, c=c: e.tensor_copy(
                        M_sb[c][96:128, 512:1024], ps_qk[1][0:32, :]))

                if DEBUG_TAPS:
                    dma("sync", sq, dbg["dbg_qf0"], qf[0][:], waits=[(sv, sv.n)])
                    dma("sync", sq, dbg["dbg_kv"], kv_sb[:], waits=[(sv, kv_ready)])
                    dma("sync", sq, dbg["dbg_ks"], ks_sb[:], waits=[(sv, kv_ready)])
                    dma("sync", sq, dbg["dbg_Kb"], Kb[:], waits=[(sv, kb_ready)])
                    dma("sync", sq, dbg["dbg_M0"], M_sb[0][:], waits=[(sv, sv.n)])

            es_x = contextlib.ExitStack()
            xn2 = es_x.enter_context(nc.sbuf_tensor("xn2", [128, NT * 1024], BF16))
            wq_bf = [es_x.enter_context(nc.sbuf_tensor(f"wq_bf{c}", [128, D], BF16))
                     for c in range(8)]
            with contextlib.ExitStack() as es_p2:
                def sb2(name, shape, dt):
                    return es_p2.enter_context(nc.sbuf_tensor(name, shape, dt))

                xr = [sb2(f"xr{i}", [128, D], F32) for i in range(2)]
                x2sb = [sb2(f"x2sb{i}", [128, D], F32) for i in range(2)]
                usb = [sb2(f"usb{i}", [128, D], F32) for i in range(2)]
                qs = sb2("qs", [128, 4096], BF16)
                wqa = sb2("wqa", [128, D], BF16)
                wqb2 = sb2("wqb2", [128, D], BF16)
                asum = sb2("asum", [128, 1], F32)
                wabs = sb2("wabs", [128, 1], F32)
                sum_s = sb2("sum_s", [1, 1], F32)
                beta_s = sb2("beta_s", [1, 1], F32)
                rbeta_s = sb2("rbeta_s", [1, 1], F32)
                dh = sb2("dh", [16, 512], F32)
                recb = [sb2("recbs", [16, 512], BF16)] * 2
                sel_sb = sb2("sel_sb", [16, D], BF16)
                gam_rep = sb2("gam_rep", [128, D], BF16)
                bet_rep = sb2("bet_rep", [128, D], BF16)
                st12b = sb2("st12b", [128, 12], F32)
                mv2 = [sb2(f"mv2_{i}", [128, 2], F32) for i in range(2)]
                veps2 = [sb2(f"veps2_{i}", [128, 1], F32) for i in range(2)]
                c_a = sb2("c_a", [128, 1], F32)
                c_b = [sb2(f"c_b{i}", [128, 1], F32) for i in range(2)]
                c_col = [sb2(f"c_col{i}", [128, 1], F32) for i in range(2)]
                rmx = sb2("rmx", [128, 1], F32)

                ps_d = es_p2.enter_context(nc.psum_tensor("ps_d", [16, 512], F32))
                ps_r = [es_p2.enter_context(nc.psum_tensor(f"ps_r{i}", [128, 512], F32))
                        for i in range(2)]
                ps_p = [es_p2.enter_context(nc.psum_tensor(f"ps_p{i}", [128, 1024], F32))
                        for i in range(2)]
                ps_s = es_p2.enter_context(nc.psum_tensor("ps_s", [128, 128], F32))

                dma("sync", sq, sel_sb[:], sel_in,
                    waits=[(sv, sv.n), (sp, sp.n), (sa, sa.n)])

                for (rsrc, dst) in ((gam_in, gam_rep), (bet_in, bet_rep)):
                    dma("sync", sq, xr[0][0:1, :], rsrc, waits=[(sp, sp.n)])
                    for piece in range(8):
                        emit("tensor", sp, lambda e, piece=piece: e.matmul(
                            ps_s[:, :], ones_f[0:1, :],
                            xr[0][0:1, piece * 128:(piece + 1) * 128],
                            start=True, stop=True, skip_group_check=True),
                            waits=[(sq, sq.n), (sv, sv.n)])
                        emit("vector", sv, lambda e, dst=dst, piece=piece: e.tensor_copy(
                            dst[:, piece * 128:(piece + 1) * 128], ps_s[:, :]),
                            waits=[(sp, sp.n)])

                emit("vector", sv, lambda e: e.memset(asum[:], 0.0))
                for c in range(8):
                    dma("sync", sq, xr[0][:], wffn[c * 128:(c + 1) * 128, :],
                        waits=[(sv, sv.n), (sp, sp.n)])
                    emit("vector", sv, lambda e: e.tensor_reduce(
                        wabs[:], xr[0][:], AX.X, ALU.add, apply_absolute_value=True),
                        waits=[(sq, sq.n)])
                    emit("vector", sv, lambda e: e.tensor_tensor(
                        asum[:], asum[:], wabs[:], ALU.add), waits=[(sv, sv.n)])
                emit("tensor", sp, lambda e: e.transpose(ps_s[0:1, 0:128], asum[:], ident_f[:]),
                     waits=[(sv, sv.n)])
                emit("vector", sv, lambda e: e.tensor_reduce(
                    sum_s[:], ps_s[0:1, 0:128], AX.X, ALU.add), waits=[(sp, sp.n)])
                emit("vector", sv, lambda e: e.tensor_scalar(
                    beta_s[:], sum_s[:], 1.0 / float(D * D), None, ALU.mult),
                    waits=[(sv, sv.n)])
                emit("vector", sv, lambda e: e.tensor_scalar(
                    rbeta_s[:], beta_s[:], 1e-5, None, ALU.add), waits=[(sv, sv.n)])
                emit("vector", sv, lambda e: e.reciprocal(rbeta_s[:], rbeta_s[:]),
                     waits=[(sv, sv.n)])
                for (src, dst) in ((beta_s, bcol), (rbeta_s, rbcol)):
                    emit("tensor", sp, lambda e, src=src: e.matmul(
                        ps_s[:, 0:1], ones_f[0:1, :], src[0:1, :],
                        start=True, stop=True, skip_group_check=True),
                        waits=[(sv, sv.n)])
                    emit("vector", sv, lambda e, dst=dst: e.tensor_copy(
                        dst[:], ps_s[:, 0:1]), waits=[(sp, sp.n)])
                for c in range(8):
                    dma("sync", sq, xr[0][:], wffn[c * 128:(c + 1) * 128, :],
                        waits=[(sv, sv.n)])
                    emit("vector", None, lambda e: e.tensor_scalar(
                        wqa[:], xr[0][:], rbcol[:], 0.5, ALU.mult, ALU.is_ge),
                        waits=[(sq, sq.n)])
                    emit("vector", None, lambda e: e.tensor_scalar(
                        wqb2[:], xr[0][:], rbcol[:], -0.5, ALU.mult, ALU.is_le))
                    emit("vector", None, lambda e: e.tensor_tensor(
                        wqa[:], wqa[:], wqb2[:], ALU.subtract))
                    emit("vector", sv, lambda e, c=c: e.tensor_scalar(
                        wq_bf[c][:], wqa[:], bcol[:], None, ALU.mult), waits=[(sv, sv.n)])
                if DEBUG_TAPS:
                    dma("sync", sq, dbg["dbg_wq0"], wq_bf[0][:], waits=[(sv, sv.n)])

                resid_done = {}
                x2w = {}
                scr_w = {}
                qs_used = {}
                poolb = {}
                act_done = {}
                aff_done = {}

                def stage_b(k):
                    emit("vector", sv, lambda e, k=k: e.tensor_copy(
                        c_col[k % 2][:], c_b[k % 2][:]), waits=[(sa, act_done[k])])
                    uw = [(sv, sv.n)]
                    if (k - 2) in poolb:
                        uw.append((sg, poolb[k - 2]))
                    emit("vector", sv, lambda e, k=k: e.tensor_scalar(
                        usb[k % 2][:], x2sb[k % 2][:], mv2[k % 2][:, 0:1],
                        c_col[k % 2][:], ALU.subtract, ALU.mult), waits=uw)
                    aff_done[k] = sv.n
                    emit("gpsimd", sg, lambda e, k=k: e.tensor_tensor(
                        usb[k % 2][:], usb[k % 2][:], gam_rep[:], ALU.mult),
                        waits=[(sv, sv.n)])
                    emit("gpsimd", sg, lambda e, k=k: e.tensor_tensor(
                        xn2[:, k * 1024:(k + 1) * 1024], usb[k % 2][:],
                        bet_rep[:], ALU.add))
                    poolb[k] = sg.n
                    if k > 0:
                        emit("vector", sv, lambda e, k=k: e.reduce_max(
                            rmx[:], xn2[:, (k - 1) * 1024: k * 1024], axis=AX.X,
                            apply_absolute_value=True), waits=[(sg, poolb[k - 1])])
                        emit("vector", sv, lambda e: e.tensor_tensor(
                            gmax[:], gmax[:], rmx[:], ALU.max), waits=[(sv, sv.n)])

                for g0 in range(2):
                    dma("sync", sxr[g0], xr[g0][:], x_in[g0 * 128:(g0 + 1) * 128, :],
                        waits=[(sv, sv.n)])
                    x2w[g0] = (sxr[g0], sxr[g0].n)
                den_ready = {}
                dh_used = {}
                bcast_done = {}

                def emit_den(g):
                    def dmm(e, g=g):
                        ins = None
                        for c in range(8):
                            ins = e.matmul(ps_d[:, :], Kb[:, c * 16:(c + 1) * 16],
                                           qf[c][:, g * 512:(g + 1) * 512],
                                           start=(c == 0), stop=(c == 7),
                                           skip_group_check=True)
                        return ins
                    dw = [(sv, kb_ready)]
                    if (g - 1) in dh_used:
                        dw.append((sv, dh_used[g - 1]))
                    emit("tensor", sp, dmm, waits=dw)
                    emit("vector", sv, lambda e: e.tensor_scalar(
                        dh[:], ps_d[:, :], 1e-6, None, ALU.max), waits=[(sp, sp.n)])
                    emit("vector", sv, lambda e: e.reciprocal(dh[:], dh[:]),
                         waits=[(sv, sv.n)])
                    rbw = [(sv, sv.n)]
                    if (g - 1) in bcast_done:
                        rbw.append((sp, bcast_done[g - 1]))
                    emit("vector", sv, lambda e, g=g: e.tensor_copy(
                        recb[g % 2][:], dh[:]), waits=rbw)
                    dh_used[g] = sv.n
                    den_ready[g] = sv.n

                emit_den(0)
                for g in range(8):
                    rec_ready = den_ready[g]
                    if DEBUG_TAPS and g == 0:
                        dma("sync", sq, dbg["dbg_rec"], dh[:], waits=[(sv, rec_ready)])
                    for c in range(8):
                        bw = [(sv, rec_ready)]
                        if (g, c - 2) in qs_used:
                            bw.append((sv, qs_used[(g, c - 2)]))
                        emit("tensor", sp, lambda e, c=c, g=g: e.matmul(
                            ps_r[c % 2][:, :], sel_sb[:, c * 128:(c + 1) * 128],
                            recb[g % 2][:], start=True, stop=True,
                            skip_group_check=True), waits=bw)
                        emit("vector", sv, lambda e, c=c, g=g: e.tensor_tensor(
                            qs[:, c * 512:(c + 1) * 512],
                            qf[c][:, g * 512:(g + 1) * 512],
                            ps_r[c % 2][:, :], ALU.mult), waits=[(sp, sp.n)], fast=True)
                        qs_used[(g, c)] = sv.n
                    bcast_done[g] = sp.n
                    qs_ready = sv.n
                    if DEBUG_TAPS and g == 0:
                        dma("sync", sq, dbg["dbg_qs"], qs[:], waits=[(sv, qs_ready)])
                    for tt in range(4):
                        gg = 4 * g + tt

                        def pmm(e, tt=tt):
                            ins = None
                            for c in range(8):
                                for nh in range(2):
                                    ins = e.matmul(
                                        ps_p[tt % 2][:, nh * 512:(nh + 1) * 512],
                                        qs[:, c * 512 + tt * 128: c * 512 + (tt + 1) * 128],
                                        M_sb[c][:, nh * 512:(nh + 1) * 512],
                                        start=(c == 0), stop=(c == 7),
                                        skip_group_check=True)
                            return ins
                        pw = [(sv, qs_ready)]
                        if (gg - 2) in resid_done:
                            pw.append((sv, resid_done[gg - 2]))
                        emit("tensor", sp, pmm, waits=pw)
                        if tt == 0 and g + 1 < 8:
                            emit_den(g + 1)
                        rw = [(sp, sp.n), x2w[gg]]
                        if (gg - 2) in scr_w:
                            rw.append(scr_w[gg - 2])
                        if (gg - 2) in aff_done:
                            rw.append((sv, aff_done[gg - 2]))
                        emit("vector", sv, lambda e, tt=tt, gg=gg: e.tensor_tensor(
                            x2sb[gg % 2][:], ps_p[tt % 2][:, :], xr[gg % 2][:], ALU.add),
                            waits=rw, fast=True)
                        resid_done[gg] = sv.n
                        if gg + 2 < NT:
                            dma("sync", sxr[gg % 2], xr[(gg + 2) % 2][:],
                                x_in[(gg + 2) * 128:(gg + 3) * 128, :],
                                waits=[(sv, resid_done[gg])])
                            x2w[gg + 2] = (sxr[gg % 2], sxr[gg % 2].n)
                        if DEBUG_TAPS and gg == 4:
                            dma("sync", sq, dbg["dbg_x2"], x2sb[0][:],
                                waits=[(sv, resid_done[4])])
                        dma("gpsimd", gw[gg % 2], x2_scr[gg * 128:(gg + 1) * 128, :],
                            x2sb[gg % 2][:], waits=[(sv, resid_done[gg])])
                        scr_w[gg] = (gw[gg % 2], gw[gg % 2].n)
                        emit("vector", sv, lambda e, gg=gg: (
                            e.bn_stats(st12b[:, 0:6], x2sb[gg % 2][:, 0:512]),
                            e.bn_stats(st12b[:, 6:12], x2sb[gg % 2][:, 512:1024]))[-1])
                        emit("vector", sv, lambda e, gg=gg: e.bn_aggr(
                            mv2[gg % 2][:], st12b[:]), waits=[(sv, sv.n)])
                        emit("vector", sv, lambda e, gg=gg: e.tensor_scalar(
                            veps2[gg % 2][:], mv2[gg % 2][:, 1:2], 1.001, 1e-8,
                            ALU.mult, ALU.add), waits=[(sv, sv.n)])
                        emit("scalar", sa, lambda e, gg=gg: e.activation(
                            c_a[:], veps2[gg % 2][:], AF.Ln), waits=[(sv, sv.n)])
                        emit("scalar", sa,
                             lambda e, gg=gg: e.activation(c_b[gg % 2][:], c_a[:],
                                                           AF.Exp, scale=-0.5),
                             waits=[(sa, sa.n)])
                        act_done[gg] = sa.n
                        if gg > 0:
                            stage_b(gg - 1)

                stage_b(31)
                emit("vector", sv, lambda e: e.reduce_max(
                    rmx[:], xn2[:, 31 * 1024: 32 * 1024], axis=AX.X,
                    apply_absolute_value=True), waits=[(sg, poolb[31])])
                emit("vector", sv, lambda e: e.tensor_tensor(
                    gmax[:], gmax[:], rmx[:], ALU.max), waits=[(sv, sv.n)])
                emit("tensor", sp, lambda e: e.transpose(ps_s[0:1, 0:128], gmax[:],
                                                         ident_f[:]), waits=[(sv, sv.n)])
                emit("vector", sv, lambda e: e.reduce_max(
                    gm_s[:], ps_s[0:1, 0:128], axis=AX.X), waits=[(sp, sp.n)])
                emit("vector", sv, lambda e: e.memset(dh[0:1, :], 0.0), waits=[(sv, sv.n)])
                emit("vector", sv, lambda e: e.tensor_copy(dh[0:1, 0:1], gm_s[:]),
                     waits=[(sv, sv.n)])
                dma("sync", sq, ccoll_i.ap(), dh[0:1, :], waits=[(sv, sv.n)])
                emit("gpsimd", None, lambda e: e.collective_compute(
                    "AllReduce", ALU.max, replica_groups=[list(range(NCORE))],
                    ins=[ccoll_i.ap().opt()], outs=[ccoll_o.ap().opt()]),
                    waits=[(sq, sq.n)])
                dma("sync", sq, dh[0:1, 0:512], ccoll_o.ap(), waits=[(sg, sg.n)])
                emit("vector", sv, lambda e: e.tensor_scalar(
                    gm_s[:], dh[0:1, 0:1], CLIPF, None, ALU.mult), waits=[(sq, sq.n)])
                emit("tensor", sp, lambda e: e.matmul(
                    ps_s[:, 0:1], ones_f[0:1, :], gm_s[0:1, :], start=True, stop=True,
                    skip_group_check=True), waits=[(sv, sv.n)])
                emit("vector", sv, lambda e: e.tensor_copy(gcol[:], ps_s[:, 0:1]),
                     waits=[(sp, sp.n)])
                emit("vector", sv, lambda e: e.tensor_scalar(
                    gneg[:], gcol[:], -1.0, None, ALU.mult), waits=[(sv, sv.n)])
                if DEBUG_TAPS:
                    dma("sync", sq, dbg["dbg_xn2"], xn2[:, 4 * 1024:5 * 1024], waits=[(sv, sv.n)])
                    dma("sync", sq, dbg["dbg_xn2all"], xn2[:, :], waits=[(sv, sv.n)])
                    dma("sync", sq, dbg["dbg_gm"], gm_s[:], waits=[(sv, sv.n)])
                p2_scr_all = (gw[0].n, gw[1].n)

            with contextlib.ExitStack() as es_p3:
                pass_marker = None
                def sb3(name, shape, dt):
                    return es_p3.enter_context(nc.sbuf_tensor(name, shape, dt))
                x2r = [sb3(f"x2r{i}", [128, D], F32) for i in range(2)]
                xq_tm = [sb3(f"xq_tm{i}", [128, D], BF16) for i in range(2)]
                xq_fm = [sb3(f"xqfm{i}", [128, D], BF16) for i in range(2)]
                out_sb = [sb3(f"outsb{i}", [128, D], F32) for i in range(2)]
                ps_T3 = [es_p3.enter_context(nc.psum_tensor(f"ps_T3{i}", [128, 1024], BF16))
                         for i in range(2)]
                ps_f = [es_p3.enter_context(nc.psum_tensor(f"ps_f{i}", [128, 1024], F32))
                        for i in range(2)]

                x2r_w = {}
                T3_mm = {}
                out_done = {}
                y_w = {}
                b4p3 = {}
                for g0 in range(2):
                    dma("sync", sx2r[g0], x2r[g0][:], x2_scr[g0 * 128:(g0 + 1) * 128, :],
                        waits=[(gw[0], p2_scr_all[0]), (gw[1], p2_scr_all[1])])
                    x2r_w[g0] = (sx2r[g0], sx2r[g0].n)
                def emit_clipT3(gg):
                    cw = [(sv, sv.n)]
                    if gg == 0:
                        cw += [(gw[0], gw[0].n), (gw[1], gw[1].n)]
                    if (gg - 2) in T3_mm:
                        cw.append((sp, T3_mm[gg - 2]))
                    emit("vector", sv, lambda e, gg=gg: e.tensor_scalar(
                        xq_tm[gg % 2][:], xn2[:, gg * 1024:(gg + 1) * 1024], gcol[:],
                        gneg[:], ALU.min, ALU.max), waits=cw, fast=True)
                    xq_ready = sv.n

                    def tr3(e, gg=gg):
                        ins = None
                        for c in range(8):
                            ins = e.transpose(ps_T3[gg % 2][:, c * 128:(c + 1) * 128],
                                              xq_tm[gg % 2][:, c * 128:(c + 1) * 128],
                                              ident_b[:])
                        return ins
                    emit("tensor", sp, tr3,
                         waits=[(sv, xq_ready)] + ([(sv, b4p3[gg % 2])] if (gg % 2) in b4p3 else []))
                    T3_mm[gg] = sp.n
                    emit("vector", sv, lambda e, gg=gg: e.tensor_copy(
                        xq_fm[gg % 2][:], ps_T3[gg % 2][:, :]), waits=[(sp, T3_mm[gg])],
                        fast=True)
                    b4p3[gg % 2] = sv.n
                    fm_done3[gg] = sv.n

                def emit_ffn(gg):
                    def fmm(e, gg=gg):
                        ins = None
                        for c in range(8):
                            for nh in range(2):
                                ins = e.matmul(
                                    ps_f[gg % 2][:, nh * 512:(nh + 1) * 512],
                                    xq_fm[gg % 2][:, c * 128:(c + 1) * 128],
                                    wq_bf[c][:, nh * 512:(nh + 1) * 512],
                                    start=(c == 0), stop=(c == 7),
                                    skip_group_check=True)
                        return ins
                    fw = [(sv, fm_done3[gg])]
                    if (gg - 2) in out_done:
                        fw.append((sv, out_done[gg - 2]))
                    emit("tensor", sp, fmm, waits=fw)
                    ow = [(sp, sp.n), x2r_w[gg]]
                    if (gg - 2) in y_w:
                        ow.append(y_w[gg - 2])
                    emit("vector", sv, lambda e, gg=gg: e.tensor_tensor(
                        out_sb[gg % 2][:], ps_f[gg % 2][:, :], x2r[gg % 2][:], ALU.add),
                        waits=ow, fast=True)
                    out_done[gg] = sv.n
                    dma("gpsimd", gy[gg % 2], y_out[gg * 128:(gg + 1) * 128, :],
                        out_sb[gg % 2][:], waits=[(sv, out_done[gg])])
                    y_w[gg] = (gy[gg % 2], gy[gg % 2].n)
                    if gg + 2 < NT:
                        dma("sync", sx2r[gg % 2], x2r[(gg + 2) % 2][:],
                            x2_scr[(gg + 2) * 128:(gg + 3) * 128, :],
                            waits=[(sv, out_done[gg])])
                        x2r_w[gg + 2] = (sx2r[gg % 2], sx2r[gg % 2].n)

                fm_done3 = {}
                emit_clipT3(0)
                for gg in range(NT):
                    if gg + 1 < NT:
                        emit_clipT3(gg + 1)
                    emit_ffn(gg)

                def fin(e):
                    e.wait_ge(gy[0].h, gy[0].n)
                    e.wait_ge(gy[1].h, gy[1].n)
                on("sync", fin)
            es_x.close()

    return nc


def kernel(L0_attn_x, L0_attn_w_qkv, L0_attn_w_out, L0_ffn_weight,
           L0_ffn_ln_gamma, L0_ffn_ln_beta):
    x = np.ascontiguousarray(np.asarray(L0_attn_x, dtype=np.float32))
    w_qkv = np.asarray(L0_attn_w_qkv, dtype=np.float32)
    w_out = np.asarray(L0_attn_w_out, dtype=np.float32)
    W = np.ascontiguousarray(np.asarray(L0_ffn_weight, dtype=np.float32))
    gam = np.asarray(L0_ffn_ln_gamma, dtype=np.float32).reshape(1, D)
    bet = np.asarray(L0_ffn_ln_beta, dtype=np.float32).reshape(1, D)

    cc_b, ss_b, perm, sel, mask2, ident_b, ident_f = _host_consts()

    wqkvT = np.empty((D, 3 * D), dtype=np.float32)
    wqkvT[:, 0:D] = w_qkv[0:D][perm].T
    wqkvT[:, D:2 * D] = w_qkv[D:2 * D][perm].T
    wqkvT[:, 2 * D:3 * D] = w_qkv[2 * D:3 * D].T
    wqkvT = np.ascontiguousarray(wqkvT)
    woutT = np.ascontiguousarray(w_out.T)

    nc = build_nc()
    shared = {
        "wqkvT": wqkvT, "woutT": woutT, "wffn": W,
        "gam_in": np.ascontiguousarray(gam), "bet_in": np.ascontiguousarray(bet),
        "cc_in": cc_b, "ss_in": ss_b, "sel_in": sel, "mask_in": mask2,
        "idb_in": ident_b, "idf_in": ident_f,
    }
    in_maps = []
    for b in range(NCORE):
        m = dict(shared)
        m["x_in"] = np.ascontiguousarray(x[b])
        in_maps.append(m)
    res = run_bass_kernel_spmd(nc, in_maps, core_ids=list(range(NCORE)))
    return np.stack([res.results[b]["y_out"] for b in range(NCORE)], axis=0)
